# version 09
# baseline (speedup 1.0000x reference)
import numpy as np
import ml_dtypes
import concourse.bass as bass
import concourse.mybir as mybir
from concourse.bass_utils import run_bass_kernel_spmd
from contextlib import ExitStack

F32 = mybir.dt.float32
BF16 = mybir.dt.bfloat16
ALU = mybir.AluOpType
AF = mybir.ActivationFunctionType
AX = mybir.AxisListType
AP = bass.AP

B, T, D, DH, H, I = 8, 2048, 2048, 128, 16, 8192
NT = T // 128
NC = D // 128
NI = I // 128
EPS = 1e-5
NEG_BIG = -1.0e38
ISQI = float(np.float32(1.0 / np.sqrt(np.float64(I))))
_sd = float(np.float64(DH) ** -0.5)
SD1 = float(np.float32(_sd))
SD2 = float(np.float32(_sd - SD1))
PI_F = float(np.float32(np.pi))
PI_DN = float(np.nextafter(np.float32(np.pi), np.float32(0)))
TWO_PI = 2.0 * np.pi
_c1 = float(np.float32(6.28125))
_c2 = float(np.float32(TWO_PI - _c1))
_c3 = float(np.float32(TWO_PI - _c1 - float(np.float32(TWO_PI - _c1))))
LN1E4 = float(np.log(np.float64(10000.0)))
_e1 = float(np.float32(-LN1E4 / 64.0))
_e2 = float(np.float32(-LN1E4 / 64.0 - _e1))

ENGINES = ("tensor", "vector", "scalar", "pool", "sync")
DMA_QUEUES = ("sync", "scalar", "pool")


class Weaver:
    NDMA = 12

    def __init__(self, nc):
        self.nc = nc
        self.ops = []
        self.cnt = {}
        self.know = {e: {} for e in ENGINES}
        self.know_direct = {e: {} for e in ENGINES}
        self.snap = {}
        self.last_w = {}
        self.readers = {}
        self.sem_names = []
        self.dma_rot = {q: 0 for q in DMA_QUEUES}
        for e in ENGINES:
            self.cnt["c_" + e] = 0
            self.sem_names.append("c_" + e)
        for q in DMA_QUEUES:
            for j in range(self.NDMA):
                self.cnt[f"d_{q}{j}"] = 0
                self.sem_names.append(f"d_{q}{j}")

    def op(self, engine, fn, reads=(), writes=(), dma=False, tag=""):
        if dma:
            j = self.dma_rot[engine]
            self.dma_rot[engine] = (j + 1) % self.NDMA
            sem = f"d_{engine}{j}"
            step = 16
        else:
            sem = "c_" + engine
            step = 1
        needed = {}
        if dma and self.cnt[sem] > 0:
            needed[sem] = self.cnt[sem]
        for r in reads:
            lw = self.last_w.get(r)
            if lw is not None and lw[1] > needed.get(lw[0], 0):
                needed[lw[0]] = lw[1]
        for wb in writes:
            lw = self.last_w.get(wb)
            if lw is not None and lw[1] > needed.get(lw[0], 0):
                needed[lw[0]] = lw[1]
            for rs, rv in self.readers.get(wb, ()):
                if rv > needed.get(rs, 0):
                    needed[rs] = rv
        know = self.know[engine]
        kd = self.know_direct[engine]
        if engine == "tensor" and not dma:
            needed.pop("c_tensor", None)
        waits = []
        for s, v in sorted(needed.items()):
            if s.startswith("d_"):
                pass
            elif know.get(s, 0) >= v:
                continue
            waits.append((s, v))
            psnap = self.snap.get((s, v))
            if psnap is not None:
                for ps, pv in psnap.items():
                    if not ps.startswith("d_") and know.get(ps, 0) < pv:
                        know[ps] = pv
            if know.get(s, 0) < v:
                know[s] = v
        self.cnt[sem] += step
        mycount = self.cnt[sem]
        self.snap[(sem, mycount)] = dict(know)
        me = (sem, mycount)
        for r in reads:
            self.readers.setdefault(r, set()).add(me)
        for wb in writes:
            self.last_w[wb] = me
            self.readers[wb] = set()
        self.ops.append((engine, fn, waits, (sem, step), tag))
        return me

    def barrier(self):
        targets = [(s, self.cnt[s]) for s in self.sem_names if self.cnt[s] > 0]
        for e in ENGINES:
            know = self.know[e]
            kd = self.know_direct[e]
            waits = []
            for s, v in targets:
                if s.startswith("d_"):
                    if kd.get(s, 0) >= v:
                        continue
                    kd[s] = v
                elif know.get(s, 0) >= v:
                    continue
                waits.append((s, v))
                if know.get(s, 0) < v:
                    know[s] = v
            if waits:
                self.ops.append((e, None, waits, None, "barrier"))

    def emit(self, ctx):
        nc = self.nc
        sems = {s: ctx.enter_context(nc.semaphore(s)) for s in self.sem_names}
        block = ctx.enter_context(nc.Block())
        by_engine = {e: [] for e in ENGINES}
        for rec in self.ops:
            by_engine[rec[0]].append(rec)

        def mk_body(engname):
            def body(eng):
                for _, fn, waits, inc, tag in by_engine[engname]:
                    for s, v in waits:
                        eng.wait_ge(sems[s], v)
                    if fn is None:
                        continue
                    inst = fn(eng)
                    if inc is not None:
                        if inst is None:
                            raise RuntimeError("op returned None: " + tag)
                        inst.then_inc(sems[inc[0]], inc[1])
            return body

        for name, starter in (("tensor", block.tensor), ("vector", block.vector),
                              ("scalar", block.scalar), ("pool", block.gpsimd),
                              ("sync", block.sync)):
            if by_engine[name]:
                starter(mk_body(name))


def hadamard(n):
    h = np.array([[1.0]], dtype=np.float32)
    while h.shape[0] < n:
        h = np.block([[h, h], [h, -h]]).astype(np.float32)
    return h


def build_consts():
    ident = np.eye(128, dtype=np.float32).astype(ml_dtypes.bfloat16)
    h128 = hadamard(128)
    h64 = hadamard(64)
    m2 = np.zeros((128, 128), dtype=np.float32)
    m2[:64, :64] = h64
    m2[64:, 64:] = h64
    tri = np.where(np.arange(128)[None, :] > np.arange(128)[:, None],
                   np.float32(NEG_BIG), np.float32(0.0)).astype(np.float32)
    return {"c_ident": ident, "c_identf": np.eye(128, dtype=np.float32),
            "c_h128": h128, "c_m2": m2, "c_tri": tri}


def build_program(dbg=False):
    nc = bass.Bass("TRN2", target_bir_lowering=False, debug=False)
    KIND = "ExternalOutput" if dbg else "Internal"
    ctx = ExitStack()
    w = Weaver(nc)

    x_d = nc.dram_tensor("x", [T, D], F32, kind="ExternalInput")
    gamma_d = nc.dram_tensor("gamma", [D], F32, kind="ExternalInput")
    wq_d = nc.dram_tensor("Wq", [D, D], F32, kind="ExternalInput")
    wkv_d = nc.dram_tensor("Wkv", [D, 2 * DH], F32, kind="ExternalInput")
    wo_d = nc.dram_tensor("Wo", [D, D], F32, kind="ExternalInput")
    wg_d = nc.dram_tensor("wg", [I, D], F32, kind="ExternalInput")
    wu_d = nc.dram_tensor("wu", [I, D], F32, kind="ExternalInput")
    wd_d = nc.dram_tensor("wd", [D, I], F32, kind="ExternalInput")
    ident_d = nc.dram_tensor("c_ident", [128, 128], BF16, kind="ExternalInput")
    identf_d = nc.dram_tensor("c_identf", [128, 128], F32, kind="ExternalInput")
    h128_d = nc.dram_tensor("c_h128", [128, 128], F32, kind="ExternalInput")
    m2_d = nc.dram_tensor("c_m2", [128, 128], F32, kind="ExternalInput")
    tri_d = nc.dram_tensor("c_tri", [128, 128], F32, kind="ExternalInput")
    out_d = nc.dram_tensor("out", [T, D], F32, kind="ExternalOutput")

    xqs_d = nc.dram_tensor("sc_xqs", [NC, 128, T], BF16, kind=KIND)
    wqb_d = nc.dram_tensor("sc_wqb", [NC, 128, D], BF16, kind=KIND)
    wob_d = nc.dram_tensor("sc_wob", [NC, 128, D], BF16, kind=KIND)
    pre_d = nc.dram_tensor("sc_pre", [H, 128, T], BF16, kind=KIND)
    s0_d = nc.dram_tensor("sc_s0", [NT, 128, I], F32, kind=KIND)
    s1_d = nc.dram_tensor("sc_s1", [NT, 64, 128, 128], F32, kind=KIND)
    s2_d = nc.dram_tensor("sc_s2", [NI, 128, T], BF16, kind=KIND)

    psf = [ctx.enter_context(nc.psum_tensor(f"psf{i}", [128, 512], F32))
           for i in range(6)]
    pbf = [ctx.enter_context(nc.psum_tensor(f"pbf{i}", [128, 1024], BF16))
           for i in range(2)]
    PSF = [f"PSF{i}" for i in range(6)]
    PBF = [f"PBF{i}" for i in range(2)]

    def sb(cx, name, shape, dt, side=None):
        return cx.enter_context(nc.sbuf_tensor(name, shape, dt, side=side))

    ident_s = sb(ctx, "ident", [128, 128], BF16)
    identf_s = sb(ctx, "identf", [128, 128], F32)
    h128_s = sb(ctx, "h128", [128, 128], F32)
    m2_s = sb(ctx, "m2", [128, 128], F32)
    tri_s = sb(ctx, "tri", [128, 128], F32)
    gam_s = sb(ctx, "gam", [128, NC], F32)
    ones_s = sb(ctx, "ones", [128, 1], F32)
    onesr_s = sb(ctx, "onesr", [1, 128], F32)
    epsv_s = sb(ctx, "epsv", [128, 1], F32)
    pio2_s = sb(ctx, "pio2", [128, 1], F32)
    smalls = sb(ctx, "smalls", [128, 48], F32)
    SM = "SMALLS"

    def vop(fn, r, wr, tag=""):
        w.op("vector", fn, reads=r, writes=wr, tag=tag)

    def sop(fn, r, wr, tag=""):
        w.op("scalar", fn, reads=r, writes=wr, tag=tag)

    def top(fn, r, wr, tag=""):
        w.op("tensor", fn, reads=r, writes=wr, tag=tag)

    def gop(fn, r, wr, tag=""):
        w.op("pool", fn, reads=r, writes=wr, tag=tag)

    def dma(q, out_ap, in_ap, r, wr, tag=""):
        w.op(q, (lambda o, i: (lambda e: e.dma_start(o, i)))(out_ap, in_ap),
             reads=r, writes=wr, dma=True, tag=tag)

    def tp(out_ap, in_ap, start, stop, r, wr, tag="tp", np_=128):
        idt = identf_s if in_ap.dtype == F32 else ident_s
        top((lambda o, i, st, sp, n_, id_: lambda e: e.matmul(o, i, id_[0:n_, 0:n_],
                                                              start=st, stop=sp,
                                                              is_transpose=True))
            (out_ap, in_ap, start, stop, np_, idt), list(r) + ["IDENT"], wr, tag)

    def quant_ladder(in_ap, in_keys, s_ap, s_keys, out_ap, out_keys,
                     scr3, skey, tag="q"):
        v0, za, bt = scr3[0], scr3[1], scr3[2]
        vop(lambda e: e.tensor_scalar(v0, in_ap, s_ap, 127.0, ALU.mult, ALU.min),
            list(in_keys) + list(s_keys), [skey], tag + ".v0")
        vop(lambda e: e.tensor_scalar(za, v0, 128.5, None, ALU.add),
            [skey], [skey], tag + ".sh")
        for c in (128.0, 64.0, 32.0, 16.0, 8.0, 4.0, 2.0, 1.0):
            vop((lambda cc: lambda e: e.tensor_scalar(bt, za, cc, -cc,
                                                      ALU.is_ge, ALU.mult))(c),
                [skey], [skey], tag + ".ge")
            vop(lambda e: e.tensor_tensor(za, za, bt, ALU.add),
                [skey], [skey], tag + ".up")
        vop(lambda e: e.scalar_tensor_tensor(out_ap, v0, 0.5, za,
                                             ALU.add, ALU.subtract),
            [skey], list(out_keys), tag + ".fin")

    dma("sync", ident_s[:], ident_d.ap(), [], ["IDENT"])
    dma("sync", identf_s[:], identf_d.ap(), [], ["IDENT"])
    dma("sync", h128_s[:], h128_d.ap(), [], ["H128"])
    dma("sync", m2_s[:], m2_d.ap(), [], ["M2"])
    dma("sync", tri_s[:], tri_d.ap(), [], ["TRI"])
    gop(lambda e: e.memset(smalls[:], 0.0), [], [SM])
    gop(lambda e: e.memset(ones_s[:], 1.0), [], ["ONES"])
    gop(lambda e: e.memset(onesr_s[:], 1.0), [], ["ONESR"])
    gop(lambda e: e.memset(epsv_s[:], EPS), [], ["EPSV"])
    gop(lambda e: e.memset(pio2_s[:], float(np.float32(np.pi / 2))), [], ["PIO2"])

    ph0 = ExitStack()
    grow = sb(ph0, "grow", [1, D], F32)
    dma("sync", grow[:], AP(gamma_d, 0, [[D, 1], [1, D]]), [], ["GROW"])
    for c in range(NC):
        tp(psf[5][:, 0:1], grow[0:1, c * 128:(c + 1) * 128], True, True,
           ["GROW"], [PSF[5]], "gam.t", np_=1)
        vop((lambda cc: lambda e: e.tensor_copy(gam_s[:, cc:cc + 1], psf[5][:, 0:1]))(c),
            [PSF[5]], ["GAM"], "gam.ev")

    phAB = ExitStack()
    cos_t = sb(phAB, "cos_t", [128, T], BF16, side="right")
    sin_t = sb(phAB, "sin_t", [128, T], BF16, side="right")
    xn2t = sb(phAB, "xn2t", [128, NC * T], BF16, side="right")

    iop = sb(ph0, "iop", [128, 1], F32)
    jcol = sb(ph0, "jcol", [128, 1], F32)
    invf = sb(ph0, "invf", [128, 1], F32)
    th_ = sb(ph0, "th", [128, T], F32)
    thn = sb(ph0, "thn", [128, T], F32)
    tb1 = sb(ph0, "tb1", [128, T], F32)
    tb2 = sb(ph0, "tb2", [128, T], F32)
    gop(lambda e: e.iota(iop[:], [[0, 1]], channel_multiplier=1,
                         allow_small_or_imprecise_dtypes=True), [], ["IOP"])
    gop(lambda e: e.iota(th_[:], [[1, T]], channel_multiplier=0,
                         allow_small_or_imprecise_dtypes=True), [], ["TH"])
    vop(lambda e: e.tensor_scalar(jcol[:], iop[:], 64.0, -64.0, ALU.is_ge, ALU.mult),
        ["IOP"], ["JC"], "ro.j1")
    vop(lambda e: e.tensor_tensor(jcol[:], iop[:], jcol[:], ALU.add),
        ["IOP", "JC"], ["JC"], "ro.j2")
    vop(lambda e: e.tensor_scalar(invf[:], jcol[:], _e2, None, ALU.mult),
        ["JC"], ["IVF"], "ro.e2")
    vop(lambda e: e.scalar_tensor_tensor(invf[:], jcol[:], _e1, invf[:],
                                         ALU.mult, ALU.add),
        ["JC", "IVF"], ["IVF"], "ro.e1")
    sop(lambda e: e.activation(invf[:], invf[:], AF.Exp), ["IVF"], ["IVF"], "ro.exp")
    vop(lambda e: e.tensor_scalar(th_[:], th_[:], invf[:], None, ALU.mult),
        ["TH", "IVF"], ["TH"], "ro.th")
    gop(lambda e: e.memset(thn[:], 0.0), [], ["THN"])
    for c in (256.0, 128.0, 64.0, 32.0, 16.0, 8.0, 4.0, 2.0, 1.0):
        thr = float(np.float32(c * TWO_PI))
        vop((lambda t_: lambda e: e.tensor_scalar(tb1[:], th_[:], t_, None, ALU.is_ge))(thr),
            ["TH"], ["TB1"], "ro.b")
        vop((lambda t_: lambda e: e.scalar_tensor_tensor(th_[:], tb1[:], -t_, th_[:],
                                                         ALU.mult, ALU.add))(thr),
            ["TB1", "TH"], ["TH"], "ro.u")
        vop((lambda cc: lambda e: e.scalar_tensor_tensor(thn[:], tb1[:], cc, thn[:],
                                                         ALU.mult, ALU.add))(c),
            ["TB1", "THN"], ["THN"], "ro.n")
    gop(lambda e: e.iota(tb2[:], [[1, T]], channel_multiplier=0,
                         allow_small_or_imprecise_dtypes=True), [], ["TB2"])
    vop(lambda e: e.tensor_scalar(tb2[:], tb2[:], invf[:], None, ALU.mult),
        ["TB2", "IVF"], ["TB2"], "ro.th2")
    for cw in (_c1, _c2, _c3):
        vop((lambda c_: lambda e: e.scalar_tensor_tensor(tb2[:], thn[:], -c_, tb2[:],
                                                         ALU.mult, ALU.add))(cw),
            ["THN", "TB2"], ["TB2"], "ro.cw")
    vop(lambda e: e.tensor_scalar(tb1[:], tb2[:], PI_F, None, ALU.is_ge),
        ["TB2"], ["TB1"], "ro.w1")
    for cw in (_c1, _c2, _c3):
        vop((lambda c_: lambda e: e.scalar_tensor_tensor(tb2[:], tb1[:], -c_, tb2[:],
                                                         ALU.mult, ALU.add))(cw),
            ["TB1", "TB2"], ["TB2"], "ro.w1c")
    vop(lambda e: e.tensor_scalar(tb1[:], tb2[:], -PI_F, None, ALU.is_le),
        ["TB2"], ["TB1"], "ro.w2")
    for cw in (_c1, _c2, _c3):
        vop((lambda c_: lambda e: e.scalar_tensor_tensor(tb2[:], tb1[:], c_, tb2[:],
                                                         ALU.mult, ALU.add))(cw),
            ["TB1", "TB2"], ["TB2"], "ro.w2c")
    vop(lambda e: e.tensor_scalar(tb2[:], tb2[:], PI_DN, -PI_DN, ALU.min, ALU.max),
        ["TB2"], ["TB2"], "ro.cl")
    sop(lambda e: e.activation(th_[:], tb2[:], AF.Sin), ["TB2"], ["TH"], "ro.sin")
    sop(lambda e: e.activation(tb1[:], tb2[:], AF.Abs), ["TB2"], ["TB1"], "ro.abs")
    sop(lambda e: e.activation(tb1[:], tb1[:], AF.Sin, bias=pio2_s[:], scale=-1.0),
        ["TB1", "PIO2"], ["TB1"], "ro.cos")
    vop(lambda e: e.tensor_scalar(th_[0:64, :], th_[0:64, :], -1.0, None, ALU.mult),
        ["TH"], ["TH"], "ro.neg")
    vop(lambda e: e.tensor_copy(cos_t[:], tb1[:]), ["TB1"], ["COS"], "ro.ck")
    vop(lambda e: e.tensor_copy(sin_t[:], th_[:]), ["TH"], ["SIN"], "ro.sk")

    w.barrier()
    ph0.close()

    phW = ExitStack()
    wtile = [sb(phW, f"wt{i}", [128, I], F32) for i in range(2)]
    WT = ["WT0", "WT1"]
    absdmp = sb(phW, "absdmp", [128, I], F32)
    acc_c = sb(phW, "acc_c", [128, 3], F32)
    asum = sb(phW, "asum", [128, 1], F32)
    scal1 = sb(phW, "scal1", [1, 4], F32)
    gop(lambda e: e.memset(acc_c[:], 0.0), [], ["ACC"])
    for dram_t, ntiles, width, col in ((wg_d, 64, D, 0), (wu_d, 64, D, 1),
                                       (wd_d, 16, I, 2)):
        for k in range(ntiles):
            bu = k % 2
            dma("sync", wtile[bu][:, :width],
                AP(dram_t, k * 128 * width, [[width, 128], [1, width]]),
                [], [WT[bu]], "st.ld")
            sop((lambda b_, wi: lambda e: e.activation(absdmp[:, :wi],
                                                       wtile[b_][:, :wi], AF.Abs,
                                                       accum_out=asum[:]))(bu, width),
                [WT[bu]], ["ABSD", "ASUM"], "st.abs")
            vop((lambda cc: lambda e: e.tensor_tensor(acc_c[:, cc:cc + 1],
                                                      acc_c[:, cc:cc + 1], asum[:],
                                                      ALU.add))(col),
                ["ASUM", "ACC"], ["ACC"], "st.acc")
    for col, nelem, (c_sw, c_nsw, c_isw) in ((0, I * D, (32, 33, 34)),
                                             (1, I * D, (35, 36, 37)),
                                             (2, D * I, (38, 39, 40))):
        top((lambda cc: lambda e: e.matmul(psf[5][0:1, 0:1], acc_c[:, cc:cc + 1],
                                           ones_s[:], start=True, stop=True))(col),
            ["ACC", "ONES"], [PSF[5]], "st.tot")
        vop((lambda ne: lambda e: e.tensor_scalar(scal1[0:1, 0:1], psf[5][0:1, 0:1],
                                                  1.0 / ne, EPS, ALU.mult, ALU.max))(nelem),
            [PSF[5]], ["SC1"], "st.mean")
        top(lambda e: e.matmul(psf[5][:, 1:2], onesr_s[:], scal1[0:1, 0:1],
                               start=True, stop=True),
            ["SC1", "ONESR"], [PSF[5]], "st.bc")
        vop((lambda a: lambda e: e.reciprocal(smalls[:, a:a + 1], psf[5][:, 1:2]))(c_sw),
            [PSF[5]], [SM], "st.sw")
        vop((lambda a, b_: lambda e: e.tensor_scalar(smalls[:, b_:b_ + 1],
                                                     smalls[:, a:a + 1], -1.0,
                                                     None, ALU.mult))(c_sw, c_nsw),
            [SM], [SM], "st.nsw")
        vop((lambda c_: lambda e: e.tensor_copy(smalls[:, c_:c_ + 1],
                                                psf[5][:, 1:2]))(c_isw),
            [PSF[5]], [SM], "st.isw")

    kvbf = sb(phAB, "kvbf", [128, NC * 2 * DH], BF16, side="right")
    wcvt = sb(phW, "wcvt", [128, D], BF16)
    for src_t, dst_t, use_gam, key in ((wq_d, wqb_d, True, "WQB"),
                                       (wo_d, wob_d, False, "WOB")):
        for c in range(NC):
            bu = c % 2
            dma("sync", wtile[bu][:, :D], AP(src_t, c * 128 * D, [[D, 128], [1, D]]),
                [], [WT[bu]], "pp.ld")
            if use_gam:
                sop((lambda b_, cc: lambda e: e.activation(wcvt[:], wtile[b_][:, :D],
                                                           AF.Copy,
                                                           scale=gam_s[:, cc:cc + 1]))(bu, c),
                    [WT[bu], "GAM"], ["WCVT"], "pp.cv")
            else:
                sop((lambda b_: lambda e: e.activation(wcvt[:], wtile[b_][:, :D],
                                                       AF.Copy))(bu),
                    [WT[bu]], ["WCVT"], "pp.cv")
            dma("pool", AP(dst_t, c * 128 * D, [[D, 128], [1, D]]), wcvt[:],
                ["WCVT"], [key + str(c)], "pp.sp")
    for c in range(NC):
        bu = c % 2
        dma("sync", wtile[bu][:, :2 * DH],
            AP(wkv_d, c * 128 * 2 * DH, [[2 * DH, 128], [1, 2 * DH]]),
            [], [WT[bu]], "pp.kv")
        sop((lambda b_, cc: lambda e: e.activation(kvbf[:, cc * 256:(cc + 1) * 256],
                                                   wtile[b_][:, :2 * DH], AF.Copy,
                                                   scale=gam_s[:, cc:cc + 1]))(bu, c),
            [WT[bu], "GAM"], ["KVBF"], "pp.cv3")
    w.barrier()
    phW.close()

    phA = ExitStack()
    xtile2 = [sb(phA, f"xtile{i}", [128, D], F32) for i in range(2)]
    xn2_ = [sb(phA, f"xn_{i}", [128, D], F32) for i in range(2)]
    bnb = sb(phA, "bnb", [128, 24], F32)
    mv = sb(phA, "mv", [128, 2], F32)
    stat2 = [sb(phA, f"stat{i}", [128, 8], F32) for i in range(2)]
    q4 = [sb(phA, f"q4_{i}", [128, D], F32) for i in range(4)]
    xqbf2 = [sb(phA, f"xqbf{i}", [128, D], BF16) for i in range(2)]
    xn2bf2 = [sb(phA, f"xn2bf{i}", [128, D], BF16) for i in range(2)]
    xqt_sb2 = [sb(phA, f"xqt_sb{i}", [128, NC * 128], BF16) for i in range(2)]

    def layer_stats(src_ap, src_keys, stat, SK, tag):
        for k in range(4):
            vop((lambda kk: lambda e: e.bn_stats(bnb[:, kk * 6:(kk + 1) * 6],
                                                 src_ap[:, kk * 512:(kk + 1) * 512]))(k),
                src_keys, ["BNB"], tag + ".bn")
        vop(lambda e: e.bn_aggr(mv[:], bnb[:]), ["BNB"], ["MV"], tag + ".ag")
        sop(lambda e: e.activation(stat[:, 4:5], mv[:, 1:2], AF.Ln, bias=epsv_s[:]),
            ["MV", "EPSV"], [SK], tag + ".ln")
        sop(lambda e: e.activation(stat[:, 0:1], stat[:, 4:5], AF.Exp, scale=-0.5),
            [SK], [SK], tag + ".ex")
        vop(lambda e: e.tensor_scalar(stat[:, 1:2], mv[:, 0:1], stat[:, 0:1], -1.0,
                                      ALU.mult, ALU.mult),
            ["MV", SK], [SK], tag + ".mr")

    for tt in range(NT):
        r = tt % 2
        xtile, xn, stat = xtile2[r], xn2_[r], stat2[r]
        xqbf, xn2bf, xqt_sb = xqbf2[r], xn2bf2[r], xqt_sb2[r]
        XT, XN, STAT = f"XT{r}", f"XN{r}", f"STAT{r}"
        XQB, XN2B, XQTS = f"XQB{r}", f"XN2B{r}", f"XQTS{r}"
        PB = PBF[r]
        pbb = pbf[r]
        dma("sync", xtile[:], AP(x_d, tt * 128 * D, [[D, 128], [1, D]]),
            [], [XT], "a.x")
        layer_stats(xtile, [XT], stat, STAT, "a.l1")
        vop((lambda x_, t_, s_: lambda e: e.tensor_scalar(x_[:], t_[:], s_[:, 0:1], s_[:, 1:2],
                                                          ALU.mult, ALU.add))(xn, xtile, stat),
            [XT, STAT], [XN], "a.xn")
        layer_stats(xn, [XN], stat, STAT, "a.l2")
        vop((lambda x2, x_, s_: lambda e: e.tensor_scalar(x2[:], x_[:], s_[:, 0:1], s_[:, 1:2],
                                                          ALU.mult, ALU.add))(xn2bf, xn, stat),
            [XN, STAT], [XN2B], "a.xn2")
        vop((lambda x_, s_: lambda e: e.tensor_reduce(s_[:, 2:3], x_[:], AX.X, ALU.max,
                                                      apply_absolute_value=True))(xn, stat),
            [XN], [STAT], "a.am")
        vop((lambda s_: lambda e: e.tensor_scalar(s_[:, 2:3], s_[:, 2:3], EPS, None, ALU.max))(stat),
            [STAT], [STAT], "a.cl")
        vop((lambda s_: lambda e: e.reciprocal(s_[:, 3:4], s_[:, 2:3]))(stat), [STAT], [STAT], "a.rc")
        vop((lambda s_: lambda e: e.tensor_scalar(s_[:, 3:4], s_[:, 3:4], 128.0, None, ALU.mult))(stat),
            [STAT], [STAT], "a.sa")
        vop((lambda c_, st_: lambda e: e.tensor_scalar(smalls[:, c_:c_ + 1], st_[:, 2:3],
                                                       1.0 / 128.0, None, ALU.mult))(tt, stat),
            [STAT], [SM], "a.rs")
        quant_ladder(xn[:], [XN], stat[:, 3:4], [STAT], xqbf[:], [XQB],
                     [q4[0][:], q4[1][:], q4[2][:], q4[3][:]], "QSC", "a.q")
        for srcb, skey, dst in ((xn2bf, XN2B, 0), (xqbf, XQB, 1)):
            for g in range(2):
                for j in range(8):
                    c = g * 8 + j
                    tp(pbb[:, j * 128:(j + 1) * 128],
                       srcb[:, c * 128:(c + 1) * 128], j == 0, j == 7,
                       [skey], [PB], "a.tp")
                if dst == 0:
                    vop((lambda gg, t_, pb_: lambda e: e.tensor_copy(
                        AP(xn2t, (gg * 8) * T + t_ * 128,
                           [[NC * T, 128], [T, 8], [1, 128]]),
                        pb_[:]))(g, tt, pbb),
                        [PB], ["XN2T"], "a.ev1")
                else:
                    vop((lambda gg, x_, pb_: lambda e: e.tensor_copy(
                        x_[:, gg * 1024:(gg + 1) * 1024], pb_[:]))(g, xqt_sb, pbb),
                        [PB], [XQTS], "a.ev2")
        dma("pool", AP(xqs_d, tt * 128 * NC * 128, [[NC * 128, 128], [1, NC * 128]]),
            xqt_sb[:], [XQTS], ["XQS"], "a.sp")
    w.barrier()
    phA.close()

    phBC = ExitStack()
    ktmp = sb(phBC, "ktmp", [128, T], F32)
    krot = sb(phBC, "krot", [128, T], F32)
    qf32_ = sb(phBC, "qf32", [128, T], F32)
    ktbf = sb(phBC, "ktbf", [128, T], BF16)
    vbf = sb(phBC, "vbf", [128, NT * DH], BF16)
    qt_bf2 = [sb(phBC, f"qt_bf{i}", [128, T], BF16) for i in range(2)]
    wqh = [sb(phBC, "wqh0", [128, NC * 128], BF16)]
    WQH = ["WQH0"]
    psb2 = [sb(phBC, f"psb{i}", [128, T], BF16) for i in range(2)]
    ptg82 = [sb(phBC, "ptg80", [128, 1024], BF16)]
    prebf2 = [sb(phBC, f"prebf{i}", [128, 512], BF16) for i in range(2)]
    zrow2 = [sb(phBC, f"zrow{i}", [128, 8], F32) for i in range(2)]
    rmrow2 = [sb(phBC, f"rmrow{i}", [128, 12], F32) for i in range(2)]
    xqtt = [sb(phBC, f"xqtt{i}", [128, NC * 128], BF16) for i in range(2)]
    XQTT_ = ["XQTTC0", "XQTTC1"]
    wgT = sb(phBC, "wgT", [128, NC * 512], BF16)
    wuT = sb(phBC, "wuT", [128, NC * 512], BF16)
    wld = [sb(phBC, "wld0", [128, D], F32)]
    WLD = ["WLD0"]
    qa_ = sb(phBC, "qa", [128, D], BF16)
    qq_ = sb(phBC, "qq", [128, D], BF16)
    sg2_ = [sb(phBC, f"sg{i}", [128, 512], F32) for i in range(2)]
    int2_ = [sb(phBC, f"inter_sb{i}", [128, 512], F32) for i in range(2)]
    dcol2_ = [sb(phBC, f"dcol{i}", [128, 2], F32) for i in range(2)]

    def rope(dst_bf, dst_key, srcr, src_key, extra_scale, tag):
        dma("pool", krot[0:64, :], srcr[64:128, :], [src_key], ["KROT"], tag + ".s1")
        dma("pool", krot[64:128, :], srcr[0:64, :], [src_key], ["KROT"], tag + ".s2")
        vop((lambda s_: lambda e: e.tensor_tensor(s_[:], s_[:], cos_t[:], ALU.mult))(srcr),
            [src_key, "COS"], [src_key], tag + ".c")
        vop(lambda e: e.tensor_tensor(krot[:], krot[:], sin_t[:], ALU.mult),
            ["KROT", "SIN"], ["KROT"], tag + ".s")
        if extra_scale is None:
            vop((lambda d_, s_: lambda e: e.tensor_tensor(d_[:], s_[:], krot[:], ALU.add))(dst_bf, srcr),
                [src_key, "KROT"], [dst_key], tag + ".a")
        else:
            vop((lambda s_: lambda e: e.tensor_tensor(qf32_[:], s_[:], krot[:], ALU.add))(srcr),
                [src_key, "KROT"], ["QF32"], tag + ".a")
            vop((lambda s_: lambda e: e.tensor_scalar(s_[:], qf32_[:], SD2, None, ALU.mult))(srcr),
                ["QF32"], [src_key], tag + ".p2")
            vop((lambda d_, s_: lambda e: e.scalar_tensor_tensor(d_[:], qf32_[:], SD1, s_[:],
                                                                 ALU.mult, ALU.add))(dst_bf, srcr),
                ["QF32", src_key], [dst_key], tag + ".p1")

    def gen_B():
        for tb in range(4):
            pk = tb % 2
            for hc in range(NC):
                top((lambda h_, t_, p_: lambda e: e.matmul(
                    psf[p_][:], kvbf[:, h_ * 256:h_ * 256 + 128],
                    AP(xn2t, h_ * T + t_ * 512, [[NC * T, 128], [1, 512]]),
                    start=(h_ == 0), stop=(h_ == NC - 1)))(hc, tb, pk),
                    ["KVBF", "XN2T"], [PSF[pk]], "b.kmm")
            sop((lambda t_, p_: lambda e: e.activation(ktmp[:, t_ * 512:(t_ + 1) * 512],
                                                       psf[p_][:], AF.Copy))(tb, pk),
                [PSF[pk]], ["KTMP0"], "b.kev")
            yield
        rope(ktbf, "KTBF", ktmp, "KTMP0", None, "b.rk")
        yield
        for tt in range(NT):
            pk = tt % 2
            for hc in range(NC):
                top((lambda h_, t_, p_: lambda e: e.matmul(
                    psf[p_][:, 0:128],
                    AP(xn2t, h_ * T + t_ * 128, [[NC * T, 128], [1, 128]]),
                    kvbf[:, h_ * 256 + 128:h_ * 256 + 256],
                    start=(h_ == 0), stop=(h_ == NC - 1)))(hc, tt, pk),
                    ["KVBF", "XN2T"], [PSF[pk]], "b.vmm")
            vop((lambda t_, p_: lambda e: e.tensor_copy(vbf[:, t_ * DH:(t_ + 1) * DH],
                                                        psf[p_][:, 0:128]))(tt, pk),
                [PSF[pk]], ["VBF"], "b.vev")
            yield

        B_SROLL = [0]
        for h in range(H):
            qt_bf = qt_bf2[h % 2]
            QT_ = f"QTBF{h % 2}"
            dma("sync", wqh[0][:],
                AP(wqb_d, h * 128, [[D, 128], [128 * D, NC], [1, 128]]),
                [f"WQB{c}" for c in range(NC)], [WQH[0]], "b.qw")
            yield
            for tb in range(4):
                pk = tb % 2
                for hc in range(NC):
                    top((lambda h_, t_, p_: lambda e: e.matmul(
                        psf[p_][:], wqh[0][:, h_ * 128:(h_ + 1) * 128],
                        AP(xn2t, h_ * T + t_ * 512, [[NC * T, 128], [1, 512]]),
                        start=(h_ == 0), stop=(h_ == NC - 1)))(hc, tb, pk),
                        [WQH[0], "XN2T"], [PSF[pk]], "b.qmm")
                sop((lambda t_, p_: lambda e: e.activation(ktmp[:, t_ * 512:(t_ + 1) * 512],
                                                           psf[p_][:], AF.Copy))(tb, pk),
                    [PSF[pk]], ["KTMP0"], "b.qev")
                yield
            rope(qt_bf, QT_, ktmp, "KTMP0", True, "b.rq")
            yield

            for qt in range(NT):
                r = qt % 2
                psb, ptg8, prebf = psb2[r], ptg82[0], prebf2[(qt // 4) % 2]
                zrow, rmrow = zrow2[r], rmrow2[r]
                PSB, PTG, PREB = f"PSB{r}", "PTG0", f"PREB{(qt // 4) % 2}"
                ZR, RM, BS = f"ZR{r}", f"RM{r}", PSB
                pvb = 2 + r
                nk = (qt + 1) * 128
                nb = (nk + 511) // 512
                for kb in range(nb):
                    bk = B_SROLL[0] % 2
                    B_SROLL[0] += 1
                    n = min(512, nk - kb * 512)
                    top((lambda q_, k_, n_, b_, qb_: lambda e: e.matmul(
                        psf[b_][:, 0:n_], qb_[:, q_ * 128:(q_ + 1) * 128],
                        ktbf[:, k_ * 512:k_ * 512 + n_], start=True, stop=True))(qt, kb, n, bk, qt_bf),
                        [QT_, "KTBF"], [PSF[bk]], "b.smm")
                    if kb == nb - 1:
                        dof = (nk - 128) - kb * 512
                        vop((lambda b_, o_: lambda e: e.tensor_tensor(
                            psf[b_][:, o_:o_ + 128], psf[b_][:, o_:o_ + 128], tri_s[:],
                            ALU.add))(bk, dof),
                            [PSF[bk], "TRI"], [PSF[bk]], "b.msk")
                    vop((lambda b_, n_, kk, s_, rm_: lambda e: e.tensor_scalar(
                        s_[:, kk * 512:kk * 512 + n_], psf[b_][:, 0:n_], 0.0, None, ALU.bypass,
                        ALU.max, accum_out=rm_[:, kk:kk + 1]))(bk, n, kb, psb, rmrow),
                        [PSF[bk]], [BS, RM], "b.rm")
                    vop((lambda rm_, kk: lambda e: e.tensor_scalar(
                        rm_[:, 4 + kk:5 + kk], rm_[:, kk:kk + 1], -1.0, None, ALU.mult))(rmrow, kb),
                        [RM], [RM], "b.rmn")
                    sop((lambda b_, n_, kk, p_, rm_, z_: lambda e: e.activation(
                        p_[:, kk * 512:kk * 512 + n_], psf[b_][:, 0:n_], AF.Exp,
                        bias=rm_[:, 4 + kk:5 + kk], accum_out=z_[:, kk:kk + 1]))(bk, n, kb, psb, rmrow, zrow),
                        [PSF[bk], RM], [PSB, ZR], "b.exp")
                vop((lambda rm_, n_: lambda e: e.tensor_reduce(
                    rm_[:, 8:9], rm_[:, 0:n_], AX.X, ALU.max))(rmrow, nb),
                    [RM], [RM], "b.gm")
                vop((lambda rm_, n_: lambda e: e.tensor_scalar(
                    rm_[:, 0:n_], rm_[:, 0:n_], rm_[:, 8:9], None, ALU.subtract))(rmrow, nb),
                    [RM], [RM], "b.dm")
                sop((lambda rm_, n_: lambda e: e.activation(rm_[:, 0:n_], rm_[:, 0:n_],
                                                            AF.Exp))(rmrow, nb),
                    [RM], [RM], "b.fk")
                vop((lambda rm_, z_, n_: lambda e: e.tensor_tensor(
                    z_[:, 0:n_], z_[:, 0:n_], rm_[:, 0:n_], ALU.mult))(rmrow, zrow, nb),
                    [RM, ZR], [ZR], "b.zf")
                if nb > 1:
                    vop((lambda z_, n_: lambda e: e.tensor_reduce(
                        z_[:, 5:6], z_[:, 0:n_], AX.X, ALU.add))(zrow, nb),
                        [ZR], [ZR], "b.zs")
                else:
                    vop((lambda z_: lambda e: e.tensor_copy(z_[:, 5:6], z_[:, 0:1]))(zrow),
                        [ZR], [ZR], "b.zs")
                vop((lambda rm_, z_: lambda e: e.reciprocal(rm_[:, 9:10], z_[:, 5:6]))(rmrow, zrow),
                    [ZR], [RM], "b.rz")
                vop((lambda rm_, n_: lambda e: e.tensor_scalar(
                    rm_[:, 0:n_], rm_[:, 0:n_], rm_[:, 9:10], None, ALU.mult))(rmrow, nb),
                    [RM], [RM], "b.fz")
                for kb in range(nb):
                    n = min(512, nk - kb * 512)
                    vop((lambda kk, n_, p_, rm_: lambda e: e.tensor_scalar(
                        p_[:, kk * 512:kk * 512 + n_], p_[:, kk * 512:kk * 512 + n_],
                        rm_[:, kk:kk + 1], None, ALU.mult))(kb, n, psb, rmrow),
                        [PSB, RM], [PSB], "b.pn")
                yield
                njb = qt + 1
                for j0 in range(0, njb, 8):
                    j1 = min(j0 + 8, njb)
                    pbb = pbf[(j0 // 8) % 2]
                    PBK = PBF[(j0 // 8) % 2]
                    for jb in range(j0, j1):
                        tp(pbb[:, (jb - j0) * 128:(jb - j0 + 1) * 128],
                           psb[:, jb * 128:(jb + 1) * 128], jb == j0, jb == j1 - 1,
                           [PSB], [PBK], "b.pt")
                    vop((lambda n_, pt_, pb_: lambda e: e.tensor_copy(pt_[:, 0:n_ * 128],
                                                                      pb_[:, 0:n_ * 128]))(j1 - j0, ptg8, pbb),
                        [PBK], [PTG], "b.ptv")
                    for jb in range(j0, j1):
                        top((lambda j_, o_, st, sp, pt_, pv_: lambda e: e.matmul(
                            psf[pv_][:, 0:128], vbf[:, j_ * DH:(j_ + 1) * DH],
                            pt_[:, o_ * 128:(o_ + 1) * 128], start=st, stop=sp))
                            (jb, jb - j0, jb == 0, jb == njb - 1, ptg8, pvb),
                            [PTG, "VBF"], [PSF[pvb]], "b.pv")
                sop((lambda q_, p_, pv_: lambda e: e.activation(p_[:, (q_ % 4) * 128:((q_ % 4) + 1) * 128],
                                                                psf[pv_][:, 0:128], AF.Copy))(qt, prebf, pvb),
                    [PSF[pvb]], [PREB], "b.pev")
                if qt % 4 == 3:
                    dma("pool", AP(pre_d, h * 128 * T + (qt // 4) * 512,
                                   [[T, 128], [1, 512]]),
                        prebf[:], [PREB], [f"PRE{h}"], "b.psp")
                yield

    def tern_quant(dst, dkey, srcb, c_sw, c_nsw, tag):
        vop((lambda b_: lambda e: e.tensor_scalar(qa_[:], wld[b_][:], smalls[:, c_sw:c_sw + 1],
                                                  0.5, ALU.mult, ALU.is_ge))(srcb),
            [WLD[srcb], SM], ["QAC"], tag + ".a")
        vop((lambda b_: lambda e: e.tensor_scalar(dst, wld[b_][:], smalls[:, c_nsw:c_nsw + 1],
                                                  0.5, ALU.mult, ALU.is_ge))(srcb),
            [WLD[srcb], SM], [dkey], tag + ".b")
        vop((lambda d_: lambda e: e.tensor_tensor(d_, qa_[:], d_, ALU.subtract))(dst),
            ["QAC", dkey], [dkey], tag + ".q")

    def gen_C():
        for hb2 in range(16):
            ob, half = hb2 // 2, hb2 % 2
            for mat, dram_t, wT, wkey, c_sw, c_nsw in ((0, wg_d, wgT, "WGT", 32, 33),
                                                       (1, wu_d, wuT, "WUT", 35, 36)):
                for ot in range(4):
                    bu = 0
                    row0 = ob * 1024 + half * 512 + ot * 128
                    dma("sync", wld[bu][:], AP(dram_t, row0 * D, [[D, 128], [1, D]]),
                        [], [WLD[bu]], "c.wl")
                    tern_quant(qq_[:], "QQC", bu, c_sw, c_nsw, "c.tq")
                    for g in range(2):
                        for j in range(8):
                            hc = g * 8 + j
                            tp(pbf[1][:, j * 128:(j + 1) * 128],
                               qq_[:, hc * 128:(hc + 1) * 128], j == 0, j == 7,
                               ["QQC"], [PBF[1]], "c.tp")
                        vop((lambda gg, o_, w_: lambda e: e.tensor_copy(
                            AP(w_, (gg * 8) * 512 + o_ * 128,
                               [[NC * 512, 128], [512, 8], [1, 128]]),
                            pbf[1][:]))(g, ot, wT),
                            [PBF[1]], [wkey], "c.ev")
                    yield
            for tt in range(NT):
                r = tt % 2
                sg, inter_sb, dcol = sg2_[r], int2_[r], dcol2_[r]
                SG, INTS, DCOL = f"SG{r}", f"INTS{r}", f"DCOL{r}"
                dma("scalar", xqtt[r][:],
                    AP(xqs_d, tt * 128 * NC * 128, [[NC * 128, 128], [1, NC * 128]]),
                    ["XQS"], [XQTT_[r]], "c.xq")
                vop((lambda t_, d_: lambda e: e.tensor_tensor(d_[:, 0:1], smalls[:, t_:t_ + 1],
                                                              smalls[:, 34:35], ALU.mult))(tt, dcol),
                    [SM], [DCOL], "c.d1")
                vop((lambda t_, d_: lambda e: e.tensor_tensor(d_[:, 1:2], smalls[:, t_:t_ + 1],
                                                              smalls[:, 37:38], ALU.mult))(tt, dcol),
                    [SM], [DCOL], "c.d2")
                yield
                for hc in range(NC):
                    top((lambda h_, r_: lambda e: e.matmul(
                        psf[4][:], xqtt[r_][:, h_ * 128:(h_ + 1) * 128],
                        wgT[:, h_ * 512:(h_ + 1) * 512],
                        start=(h_ == 0), stop=(h_ == NC - 1)))(hc, r),
                        [XQTT_[r], "WGT"], [PSF[4]], "c.gmm")
                    top((lambda h_, r_: lambda e: e.matmul(
                        psf[5][:], xqtt[r_][:, h_ * 128:(h_ + 1) * 128],
                        wuT[:, h_ * 512:(h_ + 1) * 512],
                        start=(h_ == 0), stop=(h_ == NC - 1)))(hc, r),
                        [XQTT_[r], "WUT"], [PSF[5]], "c.umm")
                sop((lambda s_, d_: lambda e: e.activation(s_[:], psf[4][:], AF.Silu,
                                                           scale=d_[:, 0:1]))(sg, dcol),
                    [PSF[4], DCOL], [SG], "c.si")
                vop((lambda i_, s_, d_: lambda e: e.scalar_tensor_tensor(
                    i_[:], psf[5][:], d_[:, 1:2],
                    s_[:], ALU.mult, ALU.mult))(inter_sb, sg, dcol),
                    [PSF[5], DCOL, SG], [INTS], "c.up")
                dma("pool", AP(s0_d, tt * 128 * I + ob * 1024 + half * 512, [[I, 128], [1, 512]]),
                    inter_sb[:], [INTS], [f"S0_{tt}_{ob}h{half}"], "c.sp")
                yield

    gb, gc = gen_B(), gen_C()
    done_b = done_c = False
    while not (done_b and done_c):
        if not done_b:
            try:
                next(gb)
            except StopIteration:
                done_b = True
        for _ in range(2):
            if not done_c:
                try:
                    next(gc)
                except StopIteration:
                    done_c = True
    w.barrier()
    phBC.close()
    phAB.close()

    phC2 = ExitStack()
    bigA2 = [sb(phC2, f"bigA{i}", [128, I], F32) for i in range(2)]
    bigB = sb(phC2, "bigB", [128, I], F32)
    bigC = sb(phC2, "bigC", [128, I], F32)
    q4b = [sb(phC2, f"q4b{i}", [128, D], F32) for i in range(4)]
    yqbf = sb(phC2, "yqbf", [128, I], BF16)
    yqt_sb = sb(phC2, "yqt_sb", [128, I], BF16)
    st22 = [sb(phC2, f"st2{i}", [128, 4], F32) for i in range(2)]

    for tt in range(NT):
        r = tt % 2
        bigA, st2 = bigA2[r], st22[r]
        BIGA, ST2 = f"BIGA{r}", f"ST2{r}"
        dma("sync", bigA[:], AP(s0_d, tt * 128 * I, [[I, 128], [1, I]]),
            [f"S0_{tt}_{ob}h{hf}" for ob in range(8) for hf in range(2)], [BIGA], "f.in")
        for cg in range(16):
            pt_ = cg % 2
            for j in range(4):
                c = cg * 4 + j
                tp(psf[pt_][:, j * 128:(j + 1) * 128], bigA[:, c * 128:(c + 1) * 128],
                   j == 0, j == 3, [BIGA], [PSF[pt_]], "f.tp")
            vop((lambda g_, p_, ba: lambda e: e.tensor_copy(bigB[:, g_ * 512:(g_ + 1) * 512],
                                                            psf[p_][:]))(cg, pt_, bigA),
                [PSF[pt_]], ["BIGB"], "f.ev1")
        for cg in range(16):
            ps_ = 2 + cg % 2
            for j in range(4):
                c = cg * 4 + j
                top((lambda c_, j_, p_: lambda e: e.matmul(
                    psf[p_][:, j_ * 128:(j_ + 1) * 128], h128_s[:],
                    bigB[:, c_ * 128:(c_ + 1) * 128], start=(j_ == 0), stop=(j_ == 3)))(c, j, ps_),
                    ["BIGB", "H128"], [PSF[ps_]], "f.s1")
            vop((lambda g_, p_: lambda e: e.tensor_copy(bigC[:, g_ * 512:(g_ + 1) * 512],
                                                        psf[p_][:]))(cg, ps_),
                [PSF[ps_]], ["BIGC"], "f.ev2")
        base = tt * 64 * 128 * 128
        dma("pool", AP(s1_d, base, [[128 * 128, 64], [128, 64], [1, 128]]),
            AP(bigC, 0, [[I, 64], [128, 64], [1, 128]]),
            ["BIGC"], ["S1A"], "f.sp1")
        dma("pool", AP(s1_d, base + 64 * 128, [[128 * 128, 64], [128, 64], [1, 128]]),
            AP(bigC, 64 * I, [[I, 64], [128, 64], [1, 128]]),
            ["BIGC"], ["S1B"], "f.sp2")
        dma("sync", AP(bigB, 0, [[I, 128], [128, 64], [1, 128]]),
            AP(s1_d, base, [[128, 128], [128 * 128, 64], [1, 128]]),
            ["S1A", "S1B"], ["BIGB"], "f.g")
        for bg in range(16):
            ps_ = 4 + bg % 2
            for j in range(4):
                b_ = bg * 4 + j
                top((lambda b2, j_, p_: lambda e: e.matmul(
                    psf[p_][:, j_ * 128:(j_ + 1) * 128],
                    bigB[:, b2 * 128:(b2 + 1) * 128], m2_s[:],
                    start=(j_ == 0), stop=(j_ == 3)))(b_, j, ps_),
                    ["BIGB", "M2"], [PSF[ps_]], "f.s2")
            vop((lambda g_, p_, ba: lambda e: e.tensor_copy(ba[:, g_ * 512:(g_ + 1) * 512],
                                                            psf[p_][:]))(bg, ps_, bigA),
                [PSF[ps_]], [BIGA], "f.ev3")
        vop((lambda s_, ba: lambda e: e.tensor_reduce(s_[:, 0:1], ba[:], AX.X, ALU.max,
                                                      apply_absolute_value=True))(st2, bigA),
            [BIGA], [ST2], "f.am")
        vop((lambda s_: lambda e: e.tensor_scalar(s_[:, 0:1], s_[:, 0:1], ISQI, EPS,
                                                  ALU.mult, ALU.max))(st2),
            [ST2], [ST2], "f.cl")
        vop((lambda s_: lambda e: e.reciprocal(s_[:, 1:2], s_[:, 0:1]))(st2),
            [ST2], [ST2], "f.rc")
        vop((lambda s_: lambda e: e.tensor_scalar(s_[:, 1:2], s_[:, 1:2], 128.0, None, ALU.mult))(st2),
            [ST2], [ST2], "f.s2s")
        vop((lambda s_: lambda e: e.tensor_scalar(s_[:, 2:3], s_[:, 1:2], ISQI, None, ALU.mult))(st2),
            [ST2], [ST2], "f.sap")
        vop((lambda c_, s_: lambda e: e.tensor_scalar(smalls[:, 16 + c_:17 + c_],
                                                      s_[:, 0:1], 1.0 / 128.0, None,
                                                      ALU.mult))(tt, st2),
            [ST2], [SM], "f.dqa")
        vop((lambda c_: lambda e: e.tensor_tensor(smalls[:, 16 + c_:17 + c_],
                                                  smalls[:, 16 + c_:17 + c_],
                                                  smalls[:, 40:41], ALU.mult))(tt),
            [SM], [SM], "f.dqb")
        for sl in range(4):
            quant_ladder(bigA[:, sl * D:(sl + 1) * D], [BIGA], st2[:, 2:3], [ST2],
                         yqbf[:, sl * D:(sl + 1) * D], ["YQBF"],
                         [q4b[0][:], q4b[1][:], q4b[2][:], q4b[3][:]], "QSC2",
                         "f.q%d" % sl)
        for bg in range(8):
            for j in range(8):
                b_ = bg * 8 + j
                tp(pbf[0][:, j * 128:(j + 1) * 128], yqbf[:, b_ * 128:(b_ + 1) * 128],
                   j == 0, j == 7, ["YQBF"], [PBF[0]], "f.ytp")
            vop((lambda g_: lambda e: e.tensor_copy(yqt_sb[:, g_ * 1024:(g_ + 1) * 1024],
                                                    pbf[0][:]))(bg),
                [PBF[0]], ["YQT"], "f.yev")
        dma("pool", AP(s2_d, tt * 128, [[128 * T, 64], [T, 64], [1, 128]]),
            AP(yqt_sb, 0, [[I, 64], [128, 64], [1, 128]]),
            ["YQT"], [f"S2_{tt}a"], "f.ysp1")
        dma("pool", AP(s2_d, 64 * T + tt * 128, [[128 * T, 64], [T, 64], [1, 128]]),
            AP(yqt_sb, 64 * I, [[I, 64], [128, 64], [1, 128]]),
            ["YQT"], [f"S2_{tt}b"], "f.ysp2")
    w.barrier()
    phC2.close()

    phD = ExitStack()
    wdT = sb(phD, "wdT", [128, NI * 512], BF16)
    wo_blk = sb(phD, "wo_blk", [128, NC * 512], BF16)
    wdld = [sb(phD, "wdld0", [128, I], F32)]
    WDLD = ["WDLD0"]
    qa2 = sb(phD, "qa2", [128, I // 2], BF16)
    qb2 = sb(phD, "qb2", [128, I // 2], BF16)
    wdq = sb(phD, "wdq", [128, I // 2], BF16)
    yq_tt = [sb(phD, f"yqtt{i}", [128, NI * 128], BF16) for i in range(2)]
    YQTT = ["YQTT0", "YQTT1"]
    pre_tt = [sb(phD, f"prett{i}", [128, H * 128], BF16) for i in range(2)]
    PRETT = ["PRETT0", "PRETT1"]
    xsl = [sb(phD, f"xsl{i}", [128, 512], F32) for i in range(2)]
    XSL = ["XSL0", "XSL1"]
    osb_ = [sb(phD, f"osb{i}", [128, 512], F32) for i in range(2)]
    osb2_ = [sb(phD, f"osb2{i}", [128, 512], F32) for i in range(2)]

    for hb in range(4):
        dma("sync", wo_blk[:],
            AP(wob_d, hb * 512, [[D, 128], [128 * D, NC], [1, 512]]),
            [f"WOB{c}" for c in range(NC)], ["WOBLK"], "d.wo")
        for r in range(4):
            bu = 0
            row0 = hb * 512 + r * 128
            dma("sync", wdld[bu][:], AP(wd_d, row0 * I, [[I, 128], [1, I]]),
                [], [WDLD[bu]], "d.wl")
            for half in range(2):
                hof = half * (I // 2)
                vop((lambda b_, o_: lambda e: e.tensor_scalar(qa2[:], wdld[b_][:, o_:o_ + I // 2],
                                                              smalls[:, 38:39], 0.5,
                                                              ALU.mult, ALU.is_ge))(bu, hof),
                    [WDLD[bu], SM], ["QA2"], "d.ta")
                vop((lambda b_, o_: lambda e: e.tensor_scalar(qb2[:], wdld[b_][:, o_:o_ + I // 2],
                                                              smalls[:, 39:40], 0.5,
                                                              ALU.mult, ALU.is_ge))(bu, hof),
                    [WDLD[bu], SM], ["QB2"], "d.tb")
                vop(lambda e: e.tensor_tensor(wdq[:], qa2[:], qb2[:], ALU.subtract),
                    ["QA2", "QB2"], ["WDQ"], "d.tq")
                for fg in range(4):
                    for j in range(8):
                        fcl = fg * 8 + j
                        tp(pbf[half][:, j * 128:(j + 1) * 128], wdq[:, fcl * 128:(fcl + 1) * 128],
                           j == 0, j == 7, ["WDQ"], [PBF[half]], "d.tp")
                    vop((lambda g_, r_, p_: lambda e: e.tensor_copy(
                        AP(wdT, (g_ * 8) * 512 + r_ * 128, [[NI * 512, 128], [512, 8], [1, 128]]),
                        p_[:]))(fg + half * 4, r, pbf[half]),
                        [PBF[half]], ["WDT"], "d.ev")
        for tt in range(NT):
            bu = tt % 2
            bp = tt % 2
            osb, osb2 = osb_[bu], osb2_[bu]
            OSB, OSB2 = f"OSB{bu}", f"OSB2{bu}"
            pdn, pwo = 2 * bu, 2 * bu + 1
            dma("sync", yq_tt[bu][:],
                AP(s2_d, tt * 128, [[T, 128], [128 * T, NI], [1, 128]]),
                [f"S2_{t2}{sfx}" for t2 in range(NT) for sfx in "ab"] if hb == 0 and tt == 0 else
                [f"S2_{tt}a", f"S2_{tt}b"],
                [YQTT[bu]], "d.yq")
            dma("sync", pre_tt[bp][:],
                AP(pre_d, tt * 128, [[T, 128], [128 * T, H], [1, 128]]),
                [f"PRE{h2}" for h2 in range(H)] if hb == 0 and tt == 0 else [],
                [PRETT[bp]], "d.pre")
            dma("scalar", xsl[bu][:],
                AP(x_d, tt * 128 * D + hb * 512, [[D, 128], [1, 512]]),
                [], [XSL[bu]], "d.x")
            for fc in range(NI):
                top((lambda f_, b_, p_: lambda e: e.matmul(
                    psf[p_][:], yq_tt[b_][:, f_ * 128:(f_ + 1) * 128],
                    wdT[:, f_ * 512:(f_ + 1) * 512],
                    start=(f_ == 0), stop=(f_ == NI - 1)))(fc, bu, pdn),
                    [YQTT[bu], "WDT"], [PSF[pdn]], "d.dmm")
            for hc in range(NC):
                top((lambda h_, b_, p_: lambda e: e.matmul(
                    psf[p_][:], pre_tt[b_][:, h_ * 128:(h_ + 1) * 128],
                    wo_blk[:, h_ * 512:(h_ + 1) * 512],
                    start=(h_ == 0), stop=(h_ == NC - 1)))(hc, bp, pwo),
                    [PRETT[bp], "WOBLK"], [PSF[pwo]], "d.omm")
            vop((lambda t_, b_, o_, p_: lambda e: e.scalar_tensor_tensor(
                o_[:], psf[p_][:], smalls[:, 16 + t_:17 + t_], xsl[b_][:],
                ALU.mult, ALU.add))(tt, bu, osb, pdn),
                [PSF[pdn], SM, XSL[bu]], [OSB], "d.o1")
            vop((lambda o1, o2, p_: lambda e: e.tensor_tensor(o2[:], o1[:], psf[p_][:], ALU.add))(osb, osb2, pwo),
                [OSB, PSF[pwo]], [OSB2], "d.o2")
            dma("pool", AP(out_d, tt * 128 * D + hb * 512, [[D, 128], [1, 512]]),
                osb2[:], [OSB2], ["OUT"], "d.out")
    w.barrier()
    phD.close()

    w.emit(ctx)
    nc._ctx_keepalive = ctx
    return nc


_NC_CACHE = []


def kernel(L0_attn_x, L0_attn_gamma, L0_attn_Wq, L0_attn_Wkv, L0_attn_Wo,
           L0_ffn_w_gate, L0_ffn_w_up, L0_ffn_w_down):
    consts = build_consts()
    if not _NC_CACHE:
        _NC_CACHE.append(build_program())
    nc = _NC_CACHE[0]
    x = np.ascontiguousarray(np.asarray(L0_attn_x, dtype=np.float32))
    common = {
        "gamma": np.ascontiguousarray(np.asarray(L0_attn_gamma, np.float32)),
        "Wq": np.ascontiguousarray(np.asarray(L0_attn_Wq, np.float32)),
        "Wkv": np.ascontiguousarray(np.asarray(L0_attn_Wkv, np.float32)),
        "Wo": np.ascontiguousarray(np.asarray(L0_attn_Wo, np.float32)),
        "wg": np.ascontiguousarray(np.asarray(L0_ffn_w_gate, np.float32)),
        "wu": np.ascontiguousarray(np.asarray(L0_ffn_w_up, np.float32)),
        "wd": np.ascontiguousarray(np.asarray(L0_ffn_w_down, np.float32)),
    }
    common.update(consts)
    in_maps = []
    for b in range(B):
        m = dict(common)
        m["x"] = np.ascontiguousarray(x[b])
        in_maps.append(m)
    res = run_bass_kernel_spmd(nc, in_maps, core_ids=list(range(B)))
    out = np.stack([np.asarray(res.results[b]["out"], np.float32) for b in range(B)],
                   axis=0)
    return out


# version 13 (final)
# speedup vs baseline: 1.1200x; 1.1200x over previous
import numpy as np
import ml_dtypes
import concourse.bass as bass
import concourse.mybir as mybir
from concourse.bass_utils import run_bass_kernel_spmd
from contextlib import ExitStack

F32 = mybir.dt.float32
BF16 = mybir.dt.bfloat16
ALU = mybir.AluOpType
AF = mybir.ActivationFunctionType
AX = mybir.AxisListType
AP = bass.AP

B, T, D, DH, H, I = 8, 2048, 2048, 128, 16, 8192
NT = T // 128
NC = D // 128
NI = I // 128
EPS = 1e-5
NEG_BIG = -1.0e38
ISQI = float(np.float32(1.0 / np.sqrt(np.float64(I))))
_sd = float(np.float64(DH) ** -0.5)
SD1 = float(np.float32(_sd))
SD2 = float(np.float32(_sd - SD1))
PI_F = float(np.float32(np.pi))
PI_DN = float(np.nextafter(np.float32(np.pi), np.float32(0)))
TWO_PI = 2.0 * np.pi
_c1 = float(np.float32(6.28125))
_c2 = float(np.float32(TWO_PI - _c1))
_c3 = float(np.float32(TWO_PI - _c1 - float(np.float32(TWO_PI - _c1))))
LN1E4 = float(np.log(np.float64(10000.0)))
_e1 = float(np.float32(-LN1E4 / 64.0))
_e2 = float(np.float32(-LN1E4 / 64.0 - _e1))

ENGINES = ("tensor", "vector", "scalar", "pool", "sync")
DMA_QUEUES = ("sync", "scalar", "pool")


class Weaver:
    NDMA = 12

    def __init__(self, nc):
        self.nc = nc
        self.ops = []
        self.cnt = {}
        self.know = {e: {} for e in ENGINES}
        self.know_direct = {e: {} for e in ENGINES}
        self.snap = {}
        self.last_w = {}
        self.readers = {}
        self.sem_names = []
        self.dma_rot = {q: 0 for q in DMA_QUEUES}
        for e in ENGINES:
            self.cnt["c_" + e] = 0
            self.sem_names.append("c_" + e)
        for q in DMA_QUEUES:
            for j in range(self.NDMA):
                self.cnt[f"d_{q}{j}"] = 0
                self.sem_names.append(f"d_{q}{j}")

    def op(self, engine, fn, reads=(), writes=(), dma=False, tag=""):
        if dma:
            j = self.dma_rot[engine]
            self.dma_rot[engine] = (j + 1) % self.NDMA
            sem = f"d_{engine}{j}"
            step = 16
        else:
            sem = "c_" + engine
            step = 1
        needed = {}
        if dma and self.cnt[sem] > 0:
            needed[sem] = self.cnt[sem]
        for r in reads:
            lw = self.last_w.get(r)
            if lw is not None and lw[1] > needed.get(lw[0], 0):
                needed[lw[0]] = lw[1]
        for wb in writes:
            lw = self.last_w.get(wb)
            if lw is not None and lw[1] > needed.get(lw[0], 0):
                needed[lw[0]] = lw[1]
            for rs, rv in self.readers.get(wb, ()):
                if rv > needed.get(rs, 0):
                    needed[rs] = rv
        know = self.know[engine]
        kd = self.know_direct[engine]
        if engine == "tensor" and not dma:
            needed.pop("c_tensor", None)
        waits = []
        for s, v in sorted(needed.items()):
            if s.startswith("d_"):
                pass
            elif know.get(s, 0) >= v:
                continue
            waits.append((s, v))
            psnap = self.snap.get((s, v))
            if psnap is not None:
                for ps, pv in psnap.items():
                    if not ps.startswith("d_") and know.get(ps, 0) < pv:
                        know[ps] = pv
            if know.get(s, 0) < v:
                know[s] = v
        self.cnt[sem] += step
        mycount = self.cnt[sem]
        self.snap[(sem, mycount)] = dict(know)
        me = (sem, mycount)
        for r in reads:
            self.readers.setdefault(r, set()).add(me)
        for wb in writes:
            self.last_w[wb] = me
            self.readers[wb] = set()
        self.ops.append((engine, fn, waits, (sem, step), tag))
        return me

    def barrier(self):
        targets = [(s, self.cnt[s]) for s in self.sem_names if self.cnt[s] > 0]
        for e in ENGINES:
            know = self.know[e]
            kd = self.know_direct[e]
            waits = []
            for s, v in targets:
                if s.startswith("d_"):
                    if kd.get(s, 0) >= v:
                        continue
                    kd[s] = v
                elif know.get(s, 0) >= v:
                    continue
                waits.append((s, v))
                if know.get(s, 0) < v:
                    know[s] = v
            if waits:
                self.ops.append((e, None, waits, None, "barrier"))

    def emit(self, ctx):
        nc = self.nc
        sems = {s: ctx.enter_context(nc.semaphore(s)) for s in self.sem_names}
        block = ctx.enter_context(nc.Block())
        by_engine = {e: [] for e in ENGINES}
        for rec in self.ops:
            by_engine[rec[0]].append(rec)

        def mk_body(engname):
            def body(eng):
                for _, fn, waits, inc, tag in by_engine[engname]:
                    for s, v in waits:
                        eng.wait_ge(sems[s], v)
                    if fn is None:
                        continue
                    inst = fn(eng)
                    if inc is not None:
                        if inst is None:
                            raise RuntimeError("op returned None: " + tag)
                        inst.then_inc(sems[inc[0]], inc[1])
            return body

        for name, starter in (("tensor", block.tensor), ("vector", block.vector),
                              ("scalar", block.scalar), ("pool", block.gpsimd),
                              ("sync", block.sync)):
            if by_engine[name]:
                starter(mk_body(name))


def hadamard(n):
    h = np.array([[1.0]], dtype=np.float32)
    while h.shape[0] < n:
        h = np.block([[h, h], [h, -h]]).astype(np.float32)
    return h


def build_consts():
    ident = np.eye(128, dtype=np.float32).astype(ml_dtypes.bfloat16)
    h128 = hadamard(128)
    h64 = hadamard(64)
    m2 = np.zeros((128, 128), dtype=np.float32)
    m2[:64, :64] = h64
    m2[64:, 64:] = h64
    tri = np.where(np.arange(128)[None, :] > np.arange(128)[:, None],
                   np.float32(NEG_BIG), np.float32(0.0)).astype(np.float32)
    return {"c_ident": ident, "c_identf": np.eye(128, dtype=np.float32),
            "c_h128": h128, "c_m2": m2, "c_tri": tri}


def build_program(dbg=False):
    nc = bass.Bass("TRN2", target_bir_lowering=False, debug=False)
    KIND = "ExternalOutput" if dbg else "Internal"
    ctx = ExitStack()
    w = Weaver(nc)

    x_d = nc.dram_tensor("x", [T, D], F32, kind="ExternalInput")
    gamma_d = nc.dram_tensor("gamma", [D], F32, kind="ExternalInput")
    wq_d = nc.dram_tensor("Wq", [D, D], F32, kind="ExternalInput")
    wkv_d = nc.dram_tensor("Wkv", [D, 2 * DH], F32, kind="ExternalInput")
    wo_d = nc.dram_tensor("Wo", [D, D], F32, kind="ExternalInput")
    wg_d = nc.dram_tensor("wg", [I, D], F32, kind="ExternalInput")
    wu_d = nc.dram_tensor("wu", [I, D], F32, kind="ExternalInput")
    wd_d = nc.dram_tensor("wd", [D, I], F32, kind="ExternalInput")
    ident_d = nc.dram_tensor("c_ident", [128, 128], BF16, kind="ExternalInput")
    identf_d = nc.dram_tensor("c_identf", [128, 128], F32, kind="ExternalInput")
    h128_d = nc.dram_tensor("c_h128", [128, 128], F32, kind="ExternalInput")
    m2_d = nc.dram_tensor("c_m2", [128, 128], F32, kind="ExternalInput")
    tri_d = nc.dram_tensor("c_tri", [128, 128], F32, kind="ExternalInput")
    out_d = nc.dram_tensor("out", [T, D], F32, kind="ExternalOutput")

    xqs_d = nc.dram_tensor("sc_xqs", [NC, 128, T], BF16, kind=KIND)
    wqb_d = nc.dram_tensor("sc_wqb", [NC, 128, D], BF16, kind=KIND)
    wob_d = nc.dram_tensor("sc_wob", [NC, 128, D], BF16, kind=KIND)
    pre_d = nc.dram_tensor("sc_pre", [H, 128, T], BF16, kind=KIND)
    s0_d = nc.dram_tensor("sc_s0", [NT, 128, I], F32, kind=KIND)
    s1_d = nc.dram_tensor("sc_s1", [NT, 64, 128, 128], F32, kind=KIND)
    s2_d = nc.dram_tensor("sc_s2", [NI, 128, T], BF16, kind=KIND)

    psf = [ctx.enter_context(nc.psum_tensor(f"psf{i}", [128, 512], F32))
           for i in range(6)]
    pbf = [ctx.enter_context(nc.psum_tensor(f"pbf{i}", [128, 1024], BF16))
           for i in range(2)]
    PSF = [f"PSF{i}" for i in range(6)]
    PBF = [f"PBF{i}" for i in range(2)]

    def sb(cx, name, shape, dt, side=None):
        return cx.enter_context(nc.sbuf_tensor(name, shape, dt, side=side))

    ident_s = sb(ctx, "ident", [128, 128], BF16)
    identf_s = sb(ctx, "identf", [128, 128], F32)
    h128_s = sb(ctx, "h128", [128, 128], F32)
    m2_s = sb(ctx, "m2", [128, 128], F32)
    tri_s = sb(ctx, "tri", [128, 128], F32)
    gam_s = sb(ctx, "gam", [128, NC], F32)
    ones_s = sb(ctx, "ones", [128, 1], F32)
    onesr_s = sb(ctx, "onesr", [1, 128], F32)
    epsv_s = sb(ctx, "epsv", [128, 1], F32)
    pio2_s = sb(ctx, "pio2", [128, 1], F32)
    smalls = sb(ctx, "smalls", [128, 48], F32)
    SM = "SMALLS"

    def vop(fn, r, wr, tag=""):
        w.op("vector", fn, reads=r, writes=wr, tag=tag)

    def sop(fn, r, wr, tag=""):
        w.op("scalar", fn, reads=r, writes=wr, tag=tag)

    def top(fn, r, wr, tag=""):
        w.op("tensor", fn, reads=r, writes=wr, tag=tag)

    def gop(fn, r, wr, tag=""):
        w.op("pool", fn, reads=r, writes=wr, tag=tag)

    def dma(q, out_ap, in_ap, r, wr, tag=""):
        w.op(q, (lambda o, i: (lambda e: e.dma_start(o, i)))(out_ap, in_ap),
             reads=r, writes=wr, dma=True, tag=tag)

    def tp(out_ap, in_ap, start, stop, r, wr, tag="tp", np_=128):
        idt = identf_s if in_ap.dtype == F32 else ident_s
        top((lambda o, i, st, sp, n_, id_: lambda e: e.matmul(o, i, id_[0:n_, 0:n_],
                                                              start=st, stop=sp,
                                                              is_transpose=True))
            (out_ap, in_ap, start, stop, np_, idt), list(r) + ["IDENT"], wr, tag)

    def quant_ladder(in_ap, in_keys, s_ap, s_keys, out_ap, out_keys,
                     scr3, skey, tag="q"):
        v0, za, bt = scr3[0], scr3[1], scr3[2]
        vop(lambda e: e.tensor_scalar(v0, in_ap, s_ap, 127.0, ALU.mult, ALU.min),
            list(in_keys) + list(s_keys), [skey], tag + ".v0")
        vop(lambda e: e.tensor_scalar(za, v0, 128.5, None, ALU.add),
            [skey], [skey], tag + ".sh")
        for c in (128.0, 64.0, 32.0, 16.0, 8.0, 4.0, 2.0, 1.0):
            vop((lambda cc: lambda e: e.tensor_scalar(bt, za, cc, -cc,
                                                      ALU.is_ge, ALU.mult))(c),
                [skey], [skey], tag + ".ge")
            vop(lambda e: e.tensor_tensor(za, za, bt, ALU.add),
                [skey], [skey], tag + ".up")
        vop(lambda e: e.scalar_tensor_tensor(out_ap, v0, 0.5, za,
                                             ALU.add, ALU.subtract),
            [skey], list(out_keys), tag + ".fin")

    dma("sync", ident_s[:], ident_d.ap(), [], ["IDENT"])
    dma("sync", identf_s[:], identf_d.ap(), [], ["IDENT"])
    dma("sync", h128_s[:], h128_d.ap(), [], ["H128"])
    dma("sync", m2_s[:], m2_d.ap(), [], ["M2"])
    dma("sync", tri_s[:], tri_d.ap(), [], ["TRI"])
    gop(lambda e: e.memset(smalls[:], 0.0), [], [SM])
    gop(lambda e: e.memset(ones_s[:], 1.0), [], ["ONES"])
    gop(lambda e: e.memset(onesr_s[:], 1.0), [], ["ONESR"])
    gop(lambda e: e.memset(epsv_s[:], EPS), [], ["EPSV"])
    gop(lambda e: e.memset(pio2_s[:], float(np.float32(np.pi / 2))), [], ["PIO2"])

    ph0 = ExitStack()
    grow = sb(ph0, "grow", [1, D], F32)
    dma("sync", grow[:], AP(gamma_d, 0, [[D, 1], [1, D]]), [], ["GROW"])
    for c in range(NC):
        tp(psf[5][:, 0:1], grow[0:1, c * 128:(c + 1) * 128], True, True,
           ["GROW"], [PSF[5]], "gam.t", np_=1)
        vop((lambda cc: lambda e: e.tensor_copy(gam_s[:, cc:cc + 1], psf[5][:, 0:1]))(c),
            [PSF[5]], ["GAM"], "gam.ev")

    phAB = ExitStack()
    cos_t = sb(phAB, "cos_t", [128, T], BF16, side="right")
    sin_t = sb(phAB, "sin_t", [128, T], BF16, side="right")
    xn2t = sb(phAB, "xn2t", [128, NC * T], BF16, side="right")

    iop = sb(ph0, "iop", [128, 1], F32)
    jcol = sb(ph0, "jcol", [128, 1], F32)
    invf = sb(ph0, "invf", [128, 1], F32)
    th_ = sb(ph0, "th", [128, T], F32)
    thn = sb(ph0, "thn", [128, T], F32)
    tb1 = sb(ph0, "tb1", [128, T], F32)
    tb2 = sb(ph0, "tb2", [128, T], F32)
    gop(lambda e: e.iota(iop[:], [[0, 1]], channel_multiplier=1,
                         allow_small_or_imprecise_dtypes=True), [], ["IOP"])
    gop(lambda e: e.iota(th_[:], [[1, T]], channel_multiplier=0,
                         allow_small_or_imprecise_dtypes=True), [], ["TH"])
    vop(lambda e: e.tensor_scalar(jcol[:], iop[:], 64.0, -64.0, ALU.is_ge, ALU.mult),
        ["IOP"], ["JC"], "ro.j1")
    vop(lambda e: e.tensor_tensor(jcol[:], iop[:], jcol[:], ALU.add),
        ["IOP", "JC"], ["JC"], "ro.j2")
    vop(lambda e: e.tensor_scalar(invf[:], jcol[:], _e2, None, ALU.mult),
        ["JC"], ["IVF"], "ro.e2")
    vop(lambda e: e.scalar_tensor_tensor(invf[:], jcol[:], _e1, invf[:],
                                         ALU.mult, ALU.add),
        ["JC", "IVF"], ["IVF"], "ro.e1")
    sop(lambda e: e.activation(invf[:], invf[:], AF.Exp), ["IVF"], ["IVF"], "ro.exp")
    vop(lambda e: e.tensor_scalar(th_[:], th_[:], invf[:], None, ALU.mult),
        ["TH", "IVF"], ["TH"], "ro.th")
    gop(lambda e: e.memset(thn[:], 0.0), [], ["THN"])
    for c in (256.0, 128.0, 64.0, 32.0, 16.0, 8.0, 4.0, 2.0, 1.0):
        thr = float(np.float32(c * TWO_PI))
        vop((lambda t_: lambda e: e.tensor_scalar(tb1[:], th_[:], t_, None, ALU.is_ge))(thr),
            ["TH"], ["TB1"], "ro.b")
        vop((lambda t_: lambda e: e.scalar_tensor_tensor(th_[:], tb1[:], -t_, th_[:],
                                                         ALU.mult, ALU.add))(thr),
            ["TB1", "TH"], ["TH"], "ro.u")
        vop((lambda cc: lambda e: e.scalar_tensor_tensor(thn[:], tb1[:], cc, thn[:],
                                                         ALU.mult, ALU.add))(c),
            ["TB1", "THN"], ["THN"], "ro.n")
    gop(lambda e: e.iota(tb2[:], [[1, T]], channel_multiplier=0,
                         allow_small_or_imprecise_dtypes=True), [], ["TB2"])
    vop(lambda e: e.tensor_scalar(tb2[:], tb2[:], invf[:], None, ALU.mult),
        ["TB2", "IVF"], ["TB2"], "ro.th2")
    for cw in (_c1, _c2, _c3):
        vop((lambda c_: lambda e: e.scalar_tensor_tensor(tb2[:], thn[:], -c_, tb2[:],
                                                         ALU.mult, ALU.add))(cw),
            ["THN", "TB2"], ["TB2"], "ro.cw")
    vop(lambda e: e.tensor_scalar(tb1[:], tb2[:], PI_F, None, ALU.is_ge),
        ["TB2"], ["TB1"], "ro.w1")
    for cw in (_c1, _c2, _c3):
        vop((lambda c_: lambda e: e.scalar_tensor_tensor(tb2[:], tb1[:], -c_, tb2[:],
                                                         ALU.mult, ALU.add))(cw),
            ["TB1", "TB2"], ["TB2"], "ro.w1c")
    vop(lambda e: e.tensor_scalar(tb1[:], tb2[:], -PI_F, None, ALU.is_le),
        ["TB2"], ["TB1"], "ro.w2")
    for cw in (_c1, _c2, _c3):
        vop((lambda c_: lambda e: e.scalar_tensor_tensor(tb2[:], tb1[:], c_, tb2[:],
                                                         ALU.mult, ALU.add))(cw),
            ["TB1", "TB2"], ["TB2"], "ro.w2c")
    vop(lambda e: e.tensor_scalar(tb2[:], tb2[:], PI_DN, -PI_DN, ALU.min, ALU.max),
        ["TB2"], ["TB2"], "ro.cl")
    sop(lambda e: e.activation(th_[:], tb2[:], AF.Sin), ["TB2"], ["TH"], "ro.sin")
    sop(lambda e: e.activation(tb1[:], tb2[:], AF.Abs), ["TB2"], ["TB1"], "ro.abs")
    sop(lambda e: e.activation(tb1[:], tb1[:], AF.Sin, bias=pio2_s[:], scale=-1.0),
        ["TB1", "PIO2"], ["TB1"], "ro.cos")
    vop(lambda e: e.tensor_scalar(th_[0:64, :], th_[0:64, :], -1.0, None, ALU.mult),
        ["TH"], ["TH"], "ro.neg")
    vop(lambda e: e.tensor_copy(cos_t[:], tb1[:]), ["TB1"], ["COS"], "ro.ck")
    vop(lambda e: e.tensor_copy(sin_t[:], th_[:]), ["TH"], ["SIN"], "ro.sk")

    w.barrier()
    ph0.close()

    phW = ExitStack()
    wtile = [sb(phW, f"wt{i}", [128, I], F32) for i in range(2)]
    WT = ["WT0", "WT1"]
    absdmp = sb(phW, "absdmp", [128, I], F32)
    acc_c = sb(phW, "acc_c", [128, 3], F32)
    asum = sb(phW, "asum", [128, 1], F32)
    scal1 = sb(phW, "scal1", [1, 4], F32)
    gop(lambda e: e.memset(acc_c[:], 0.0), [], ["ACC"])
    for dram_t, ntiles, width, col in ((wg_d, 64, D, 0), (wu_d, 64, D, 1),
                                       (wd_d, 16, I, 2)):
        for k in range(ntiles):
            bu = k % 2
            dma("sync", wtile[bu][:, :width],
                AP(dram_t, k * 128 * width, [[width, 128], [1, width]]),
                [], [WT[bu]], "st.ld")
            sop((lambda b_, wi: lambda e: e.activation(absdmp[:, :wi],
                                                       wtile[b_][:, :wi], AF.Abs,
                                                       accum_out=asum[:]))(bu, width),
                [WT[bu]], ["ABSD", "ASUM"], "st.abs")
            vop((lambda cc: lambda e: e.tensor_tensor(acc_c[:, cc:cc + 1],
                                                      acc_c[:, cc:cc + 1], asum[:],
                                                      ALU.add))(col),
                ["ASUM", "ACC"], ["ACC"], "st.acc")
    for col, nelem, (c_sw, c_nsw, c_isw) in ((0, I * D, (32, 33, 34)),
                                             (1, I * D, (35, 36, 37)),
                                             (2, D * I, (38, 39, 40))):
        top((lambda cc: lambda e: e.matmul(psf[5][0:1, 0:1], acc_c[:, cc:cc + 1],
                                           ones_s[:], start=True, stop=True))(col),
            ["ACC", "ONES"], [PSF[5]], "st.tot")
        vop((lambda ne: lambda e: e.tensor_scalar(scal1[0:1, 0:1], psf[5][0:1, 0:1],
                                                  1.0 / ne, EPS, ALU.mult, ALU.max))(nelem),
            [PSF[5]], ["SC1"], "st.mean")
        top(lambda e: e.matmul(psf[5][:, 1:2], onesr_s[:], scal1[0:1, 0:1],
                               start=True, stop=True),
            ["SC1", "ONESR"], [PSF[5]], "st.bc")
        vop((lambda a: lambda e: e.reciprocal(smalls[:, a:a + 1], psf[5][:, 1:2]))(c_sw),
            [PSF[5]], [SM], "st.sw")
        vop((lambda a, b_: lambda e: e.tensor_scalar(smalls[:, b_:b_ + 1],
                                                     smalls[:, a:a + 1], -1.0,
                                                     None, ALU.mult))(c_sw, c_nsw),
            [SM], [SM], "st.nsw")
        vop((lambda c_: lambda e: e.tensor_copy(smalls[:, c_:c_ + 1],
                                                psf[5][:, 1:2]))(c_isw),
            [PSF[5]], [SM], "st.isw")

    kvbf = sb(phAB, "kvbf", [128, NC * 2 * DH], BF16, side="right")
    wcvt = sb(phW, "wcvt", [128, D], BF16)
    for src_t, dst_t, use_gam, key in ((wq_d, wqb_d, True, "WQB"),
                                       (wo_d, wob_d, False, "WOB")):
        for c in range(NC):
            bu = c % 2
            dma("sync", wtile[bu][:, :D], AP(src_t, c * 128 * D, [[D, 128], [1, D]]),
                [], [WT[bu]], "pp.ld")
            if use_gam:
                sop((lambda b_, cc: lambda e: e.activation(wcvt[:], wtile[b_][:, :D],
                                                           AF.Copy,
                                                           scale=gam_s[:, cc:cc + 1]))(bu, c),
                    [WT[bu], "GAM"], ["WCVT"], "pp.cv")
            else:
                sop((lambda b_: lambda e: e.activation(wcvt[:], wtile[b_][:, :D],
                                                       AF.Copy))(bu),
                    [WT[bu]], ["WCVT"], "pp.cv")
            dma("pool", AP(dst_t, c * 128 * D, [[D, 128], [1, D]]), wcvt[:],
                ["WCVT"], [key + str(c)], "pp.sp")
    for c in range(NC):
        bu = c % 2
        dma("sync", wtile[bu][:, :2 * DH],
            AP(wkv_d, c * 128 * 2 * DH, [[2 * DH, 128], [1, 2 * DH]]),
            [], [WT[bu]], "pp.kv")
        sop((lambda b_, cc: lambda e: e.activation(kvbf[:, cc * 256:(cc + 1) * 256],
                                                   wtile[b_][:, :2 * DH], AF.Copy,
                                                   scale=gam_s[:, cc:cc + 1]))(bu, c),
            [WT[bu], "GAM"], ["KVBF"], "pp.cv3")
    w.barrier()
    phW.close()

    phA = ExitStack()
    xtile2 = [sb(phA, f"xtile{i}", [128, D], F32) for i in range(2)]
    xn2_ = [sb(phA, f"xn_{i}", [128, D], F32) for i in range(2)]
    bnb = sb(phA, "bnb", [128, 24], F32)
    mv = sb(phA, "mv", [128, 2], F32)
    stat2 = [sb(phA, f"stat{i}", [128, 8], F32) for i in range(2)]
    q4 = [sb(phA, f"q4_{i}", [128, D], F32) for i in range(4)]
    xqbf2 = [sb(phA, f"xqbf{i}", [128, D], BF16) for i in range(2)]
    xn2bf2 = [sb(phA, f"xn2bf{i}", [128, D], BF16) for i in range(2)]
    xqt_sb2 = [sb(phA, f"xqt_sb{i}", [128, NC * 128], BF16) for i in range(2)]

    def layer_stats(src_ap, src_keys, stat, SK, tag):
        for k in range(4):
            vop((lambda kk: lambda e: e.bn_stats(bnb[:, kk * 6:(kk + 1) * 6],
                                                 src_ap[:, kk * 512:(kk + 1) * 512]))(k),
                src_keys, ["BNB"], tag + ".bn")
        vop(lambda e: e.bn_aggr(mv[:], bnb[:]), ["BNB"], ["MV"], tag + ".ag")
        sop(lambda e: e.activation(stat[:, 4:5], mv[:, 1:2], AF.Ln, bias=epsv_s[:]),
            ["MV", "EPSV"], [SK], tag + ".ln")
        sop(lambda e: e.activation(stat[:, 0:1], stat[:, 4:5], AF.Exp, scale=-0.5),
            [SK], [SK], tag + ".ex")
        vop(lambda e: e.tensor_scalar(stat[:, 1:2], mv[:, 0:1], stat[:, 0:1], -1.0,
                                      ALU.mult, ALU.mult),
            ["MV", SK], [SK], tag + ".mr")

    for tt in range(NT):
        r = tt % 2
        xtile, xn, stat = xtile2[r], xn2_[r], stat2[r]
        xqbf, xn2bf, xqt_sb = xqbf2[r], xn2bf2[r], xqt_sb2[r]
        XT, XN, STAT = f"XT{r}", f"XN{r}", f"STAT{r}"
        XQB, XN2B, XQTS = f"XQB{r}", f"XN2B{r}", f"XQTS{r}"
        PB = PBF[r]
        pbb = pbf[r]
        dma("sync", xtile[:], AP(x_d, tt * 128 * D, [[D, 128], [1, D]]),
            [], [XT], "a.x")
        layer_stats(xtile, [XT], stat, STAT, "a.l1")
        vop((lambda x_, t_, s_: lambda e: e.tensor_scalar(x_[:], t_[:], s_[:, 0:1], s_[:, 1:2],
                                                          ALU.mult, ALU.add))(xn, xtile, stat),
            [XT, STAT], [XN], "a.xn")
        layer_stats(xn, [XN], stat, STAT, "a.l2")
        vop((lambda x2, x_, s_: lambda e: e.tensor_scalar(x2[:], x_[:], s_[:, 0:1], s_[:, 1:2],
                                                          ALU.mult, ALU.add))(xn2bf, xn, stat),
            [XN, STAT], [XN2B], "a.xn2")
        vop((lambda x_, s_: lambda e: e.tensor_reduce(s_[:, 2:3], x_[:], AX.X, ALU.max,
                                                      apply_absolute_value=True))(xn, stat),
            [XN], [STAT], "a.am")
        vop((lambda s_: lambda e: e.tensor_scalar(s_[:, 2:3], s_[:, 2:3], EPS, None, ALU.max))(stat),
            [STAT], [STAT], "a.cl")
        vop((lambda s_: lambda e: e.reciprocal(s_[:, 3:4], s_[:, 2:3]))(stat), [STAT], [STAT], "a.rc")
        vop((lambda s_: lambda e: e.tensor_scalar(s_[:, 3:4], s_[:, 3:4], 128.0, None, ALU.mult))(stat),
            [STAT], [STAT], "a.sa")
        vop((lambda c_, st_: lambda e: e.tensor_scalar(smalls[:, c_:c_ + 1], st_[:, 2:3],
                                                       1.0 / 128.0, None, ALU.mult))(tt, stat),
            [STAT], [SM], "a.rs")
        quant_ladder(xn[:], [XN], stat[:, 3:4], [STAT], xqbf[:], [XQB],
                     [q4[0][:], q4[1][:], q4[2][:], q4[3][:]], "QSC", "a.q")
        for srcb, skey, dst in ((xn2bf, XN2B, 0), (xqbf, XQB, 1)):
            for g in range(2):
                for j in range(8):
                    c = g * 8 + j
                    tp(pbb[:, j * 128:(j + 1) * 128],
                       srcb[:, c * 128:(c + 1) * 128], j == 0, j == 7,
                       [skey], [PB], "a.tp")
                if dst == 0:
                    vop((lambda gg, t_, pb_: lambda e: e.tensor_copy(
                        AP(xn2t, (gg * 8) * T + t_ * 128,
                           [[NC * T, 128], [T, 8], [1, 128]]),
                        pb_[:]))(g, tt, pbb),
                        [PB], ["XN2T"], "a.ev1")
                else:
                    vop((lambda gg, x_, pb_: lambda e: e.tensor_copy(
                        x_[:, gg * 1024:(gg + 1) * 1024], pb_[:]))(g, xqt_sb, pbb),
                        [PB], [XQTS], "a.ev2")
        dma("pool", AP(xqs_d, tt * 128 * NC * 128, [[NC * 128, 128], [1, NC * 128]]),
            xqt_sb[:], [XQTS], ["XQS"], "a.sp")
    w.barrier()
    phA.close()

    phBC = ExitStack()
    ktmp = sb(phBC, "ktmp", [128, T], F32)
    krot = sb(phBC, "krot", [128, T], F32)
    ktbf = sb(phBC, "ktbf", [128, T], BF16)
    vbf = sb(phBC, "vbf", [128, NT * DH], BF16)
    qt_bf2 = [sb(phBC, f"qt_bf{i}", [128, T], BF16) for i in range(2)]
    wqh = [sb(phBC, "wqh0", [128, NC * 128], BF16)]
    WQH = ["WQH0"]
    psb2 = [sb(phBC, f"psb{i}", [128, T], BF16) for i in range(2)]
    ptg82 = [sb(phBC, f"ptg8{i}", [128, 1024], BF16) for i in range(2)]
    prebf2 = [sb(phBC, f"prebf{i}", [128, 512], BF16) for i in range(2)]
    zrow2 = [sb(phBC, f"zrow{i}", [128, 8], F32) for i in range(2)]
    rmrow2 = [sb(phBC, f"rmrow{i}", [128, 12], F32) for i in range(2)]
    xqtt = [sb(phBC, f"xqtt{i}", [128, NC * 128], BF16) for i in range(2)]
    XQTT_ = ["XQTTC0", "XQTTC1"]
    wgT = sb(phBC, "wgT", [128, NC * 512], BF16)
    wuT = sb(phBC, "wuT", [128, NC * 512], BF16)
    wld = [sb(phBC, f"wld{i}", [128, D], F32) for i in range(2)]
    WLD = ["WLD0", "WLD1"]
    qa_ = sb(phBC, "qa", [128, D], BF16)
    _qq = sb(phBC, "qq", [128, D], BF16)
    qq2 = [_qq, _qq]
    sg2_ = [sb(phBC, f"sg{i}", [128, 512], F32) for i in range(2)]
    int2_ = [sb(phBC, f"inter_sb{i}", [128, 512], F32) for i in range(2)]
    dcol2_ = [sb(phBC, f"dcol{i}", [128, 2], F32) for i in range(2)]

    def rope(dst_bf, dst_key, srcr, src_key, extra_scale, tag):
        dma("pool", krot[0:64, :], srcr[64:128, :], [src_key], ["KROT"], tag + ".s1")
        dma("pool", krot[64:128, :], srcr[0:64, :], [src_key], ["KROT"], tag + ".s2")
        vop((lambda s_: lambda e: e.tensor_tensor(s_[:], s_[:], cos_t[:], ALU.mult))(srcr),
            [src_key, "COS"], [src_key], tag + ".c")
        vop(lambda e: e.tensor_tensor(krot[:], krot[:], sin_t[:], ALU.mult),
            ["KROT", "SIN"], ["KROT"], tag + ".s")
        if extra_scale is None:
            vop((lambda d_, s_: lambda e: e.tensor_tensor(d_[:], s_[:], krot[:], ALU.add))(dst_bf, srcr),
                [src_key, "KROT"], [dst_key], tag + ".a")
        else:
            vop((lambda s_: lambda e: e.tensor_tensor(krot[:], s_[:], krot[:], ALU.add))(srcr),
                [src_key, "KROT"], ["KROT"], tag + ".a")
            vop((lambda s_: lambda e: e.tensor_scalar(s_[:], krot[:], SD2, None, ALU.mult))(srcr),
                ["KROT"], [src_key], tag + ".p2")
            vop((lambda d_, s_: lambda e: e.scalar_tensor_tensor(d_[:], krot[:], SD1, s_[:],
                                                                 ALU.mult, ALU.add))(dst_bf, srcr),
                ["KROT", src_key], [dst_key], tag + ".p1")

    def gen_B():
        for tb in range(4):
            pk = tb % 2
            for hc in range(NC):
                top((lambda h_, t_, p_: lambda e: e.matmul(
                    psf[p_][:], kvbf[:, h_ * 256:h_ * 256 + 128],
                    AP(xn2t, h_ * T + t_ * 512, [[NC * T, 128], [1, 512]]),
                    start=(h_ == 0), stop=(h_ == NC - 1)))(hc, tb, pk),
                    ["KVBF", "XN2T"], [PSF[pk]], "b.kmm")
            sop((lambda t_, p_: lambda e: e.activation(ktmp[:, t_ * 512:(t_ + 1) * 512],
                                                       psf[p_][:], AF.Copy))(tb, pk),
                [PSF[pk]], ["KTMP0"], "b.kev")
            yield
        rope(ktbf, "KTBF", ktmp, "KTMP0", None, "b.rk")
        yield
        for tt in range(NT):
            pk = tt % 2
            for hc in range(NC):
                top((lambda h_, t_, p_: lambda e: e.matmul(
                    psf[p_][:, 0:128],
                    AP(xn2t, h_ * T + t_ * 128, [[NC * T, 128], [1, 128]]),
                    kvbf[:, h_ * 256 + 128:h_ * 256 + 256],
                    start=(h_ == 0), stop=(h_ == NC - 1)))(hc, tt, pk),
                    ["KVBF", "XN2T"], [PSF[pk]], "b.vmm")
            vop((lambda t_, p_: lambda e: e.tensor_copy(vbf[:, t_ * DH:(t_ + 1) * DH],
                                                        psf[p_][:, 0:128]))(tt, pk),
                [PSF[pk]], ["VBF"], "b.vev")
            yield

        B_SROLL = [0]
        for h in range(H):
            qt_bf = qt_bf2[h % 2]
            QT_ = f"QTBF{h % 2}"
            dma("sync", wqh[0][:],
                AP(wqb_d, h * 128, [[D, 128], [128 * D, NC], [1, 128]]),
                [f"WQB{c}" for c in range(NC)], [WQH[0]], "b.qw")
            yield
            for tb in range(4):
                pk = tb % 2
                for hc in range(NC):
                    top((lambda h_, t_, p_: lambda e: e.matmul(
                        psf[p_][:], wqh[0][:, h_ * 128:(h_ + 1) * 128],
                        AP(xn2t, h_ * T + t_ * 512, [[NC * T, 128], [1, 512]]),
                        start=(h_ == 0), stop=(h_ == NC - 1)))(hc, tb, pk),
                        [WQH[0], "XN2T"], [PSF[pk]], "b.qmm")
                sop((lambda t_, p_: lambda e: e.activation(ktmp[:, t_ * 512:(t_ + 1) * 512],
                                                           psf[p_][:], AF.Copy))(tb, pk),
                    [PSF[pk]], ["KTMP0"], "b.qev")
                yield
            rope(qt_bf, QT_, ktmp, "KTMP0", True, "b.rq")
            yield

            for qt in range(NT):
                r = qt % 2
                psb, prebf = psb2[r], prebf2[(qt // 4) % 2]
                zrow, rmrow = zrow2[r], rmrow2[r]
                PSB, PREB = f"PSB{r}", f"PREB{(qt // 4) % 2}"
                ZR, RM, BS = f"ZR{r}", f"RM{r}", PSB
                pvb = 2 + r
                nk = (qt + 1) * 128
                nb = (nk + 511) // 512
                for kb in range(nb):
                    bk = B_SROLL[0] % 2
                    B_SROLL[0] += 1
                    n = min(512, nk - kb * 512)
                    top((lambda q_, k_, n_, b_, qb_: lambda e: e.matmul(
                        psf[b_][:, 0:n_], qb_[:, q_ * 128:(q_ + 1) * 128],
                        ktbf[:, k_ * 512:k_ * 512 + n_], start=True, stop=True))(qt, kb, n, bk, qt_bf),
                        [QT_, "KTBF"], [PSF[bk]], "b.smm")
                    if kb == nb - 1:
                        dof = (nk - 128) - kb * 512
                        vop((lambda b_, o_: lambda e: e.tensor_tensor(
                            psf[b_][:, o_:o_ + 128], psf[b_][:, o_:o_ + 128], tri_s[:],
                            ALU.add))(bk, dof),
                            [PSF[bk], "TRI"], [PSF[bk]], "b.msk")
                    vop((lambda b_, n_, kk, s_, rm_: lambda e: e.tensor_scalar(
                        s_[:, kk * 512:kk * 512 + n_], psf[b_][:, 0:n_], 0.0, None, ALU.bypass,
                        ALU.max, accum_out=rm_[:, kk:kk + 1]))(bk, n, kb, psb, rmrow),
                        [PSF[bk]], [BS, RM], "b.rm")
                    vop((lambda rm_, kk: lambda e: e.tensor_scalar(
                        rm_[:, 4 + kk:5 + kk], rm_[:, kk:kk + 1], -1.0, None, ALU.mult))(rmrow, kb),
                        [RM], [RM], "b.rmn")
                    sop((lambda b_, n_, kk, p_, rm_, z_: lambda e: e.activation(
                        p_[:, kk * 512:kk * 512 + n_], psf[b_][:, 0:n_], AF.Exp,
                        bias=rm_[:, 4 + kk:5 + kk], accum_out=z_[:, kk:kk + 1]))(bk, n, kb, psb, rmrow, zrow),
                        [PSF[bk], RM], [PSB, ZR], "b.exp")
                vop((lambda rm_, n_: lambda e: e.tensor_reduce(
                    rm_[:, 8:9], rm_[:, 0:n_], AX.X, ALU.max))(rmrow, nb),
                    [RM], [RM], "b.gm")
                vop((lambda rm_, n_: lambda e: e.tensor_scalar(
                    rm_[:, 0:n_], rm_[:, 0:n_], rm_[:, 8:9], None, ALU.subtract))(rmrow, nb),
                    [RM], [RM], "b.dm")
                sop((lambda rm_, n_: lambda e: e.activation(rm_[:, 0:n_], rm_[:, 0:n_],
                                                            AF.Exp))(rmrow, nb),
                    [RM], [RM], "b.fk")
                vop((lambda rm_, z_, n_: lambda e: e.tensor_tensor(
                    z_[:, 0:n_], z_[:, 0:n_], rm_[:, 0:n_], ALU.mult))(rmrow, zrow, nb),
                    [RM, ZR], [ZR], "b.zf")
                if nb > 1:
                    vop((lambda z_, n_: lambda e: e.tensor_reduce(
                        z_[:, 5:6], z_[:, 0:n_], AX.X, ALU.add))(zrow, nb),
                        [ZR], [ZR], "b.zs")
                else:
                    vop((lambda z_: lambda e: e.tensor_copy(z_[:, 5:6], z_[:, 0:1]))(zrow),
                        [ZR], [ZR], "b.zs")
                vop((lambda rm_, z_: lambda e: e.reciprocal(rm_[:, 9:10], z_[:, 5:6]))(rmrow, zrow),
                    [ZR], [RM], "b.rz")
                vop((lambda rm_, n_: lambda e: e.tensor_scalar(
                    rm_[:, 0:n_], rm_[:, 0:n_], rm_[:, 9:10], None, ALU.mult))(rmrow, nb),
                    [RM], [RM], "b.fz")
                for kb in range(nb):
                    n = min(512, nk - kb * 512)
                    vop((lambda kk, n_, p_, rm_: lambda e: e.tensor_scalar(
                        p_[:, kk * 512:kk * 512 + n_], p_[:, kk * 512:kk * 512 + n_],
                        rm_[:, kk:kk + 1], None, ALU.mult))(kb, n, psb, rmrow),
                        [PSB, RM], [PSB], "b.pn")
                yield
                njb = qt + 1
                for j0 in range(0, njb, 8):
                    j1 = min(j0 + 8, njb)
                    pbb = pbf[(j0 // 8) % 2]
                    PBK = PBF[(j0 // 8) % 2]
                    ptg8 = ptg82[(qt + j0 // 8) % 2]
                    PTG = f"PTG{(qt + j0 // 8) % 2}"
                    for jb in range(j0, j1):
                        tp(pbb[:, (jb - j0) * 128:(jb - j0 + 1) * 128],
                           psb[:, jb * 128:(jb + 1) * 128], jb == j0, jb == j1 - 1,
                           [PSB], [PBK], "b.pt")
                    vop((lambda n_, pt_, pb_: lambda e: e.tensor_copy(pt_[:, 0:n_ * 128],
                                                                      pb_[:, 0:n_ * 128]))(j1 - j0, ptg8, pbb),
                        [PBK], [PTG], "b.ptv")
                    for jb in range(j0, j1):
                        top((lambda j_, o_, st, sp, pt_, pv_: lambda e: e.matmul(
                            psf[pv_][:, 0:128], vbf[:, j_ * DH:(j_ + 1) * DH],
                            pt_[:, o_ * 128:(o_ + 1) * 128], start=st, stop=sp))
                            (jb, jb - j0, jb == 0, jb == njb - 1, ptg8, pvb),
                            [PTG, "VBF"], [PSF[pvb]], "b.pv")
                sop((lambda q_, p_, pv_: lambda e: e.activation(p_[:, (q_ % 4) * 128:((q_ % 4) + 1) * 128],
                                                                psf[pv_][:, 0:128], AF.Copy))(qt, prebf, pvb),
                    [PSF[pvb]], [PREB], "b.pev")
                if qt % 4 == 3:
                    dma("pool", AP(pre_d, h * 128 * T + (qt // 4) * 512,
                                   [[T, 128], [1, 512]]),
                        prebf[:], [PREB], [f"PRE{h}"], "b.psp")
                yield

    def tern_quant(dst, dkey, srcb, c_sw, c_nsw, tag):
        vop((lambda b_: lambda e: e.tensor_scalar(qa_[:], wld[b_][:], smalls[:, c_sw:c_sw + 1],
                                                  0.5, ALU.mult, ALU.is_ge))(srcb),
            [WLD[srcb], SM], ["QAC"], tag + ".a")
        vop((lambda b_: lambda e: e.tensor_scalar(dst, wld[b_][:], smalls[:, c_nsw:c_nsw + 1],
                                                  0.5, ALU.mult, ALU.is_ge))(srcb),
            [WLD[srcb], SM], [dkey], tag + ".b")
        vop((lambda d_: lambda e: e.tensor_tensor(d_, qa_[:], d_, ALU.subtract))(dst),
            ["QAC", dkey], [dkey], tag + ".q")

    def gen_C():
        for hb2 in range(16):
            ob, half = hb2 // 2, hb2 % 2
            for mat, dram_t, wT, wkey, c_sw, c_nsw in ((0, wg_d, wgT, "WGT", 32, 33),
                                                       (1, wu_d, wuT, "WUT", 35, 36)):
                for ot in range(4):
                    bu = (hb2 * 8 + mat * 4 + ot) % 2
                    qq_b, QQK = qq2[bu], "QQC"
                    row0 = ob * 1024 + half * 512 + ot * 128
                    dma("sync", wld[bu][:], AP(dram_t, row0 * D, [[D, 128], [1, D]]),
                        [], [WLD[bu]], "c.wl")
                    tern_quant(qq_b[:], QQK, bu, c_sw, c_nsw, "c.tq")
                    for g in range(2):
                        for j in range(8):
                            hc = g * 8 + j
                            tp(pbf[1][:, j * 128:(j + 1) * 128],
                               qq_b[:, hc * 128:(hc + 1) * 128], j == 0, j == 7,
                               [QQK], [PBF[1]], "c.tp")
                        vop((lambda gg, o_, w_: lambda e: e.tensor_copy(
                            AP(w_, (gg * 8) * 512 + o_ * 128,
                               [[NC * 512, 128], [512, 8], [1, 128]]),
                            pbf[1][:]))(g, ot, wT),
                            [PBF[1]], [wkey], "c.ev")
                    yield
            for tt in range(NT):
                r = tt % 2
                sg, inter_sb, dcol = sg2_[r], int2_[r], dcol2_[r]
                SG, INTS, DCOL = f"SG{r}", f"INTS{r}", f"DCOL{r}"
                dma("scalar", xqtt[r][:],
                    AP(xqs_d, tt * 128 * NC * 128, [[NC * 128, 128], [1, NC * 128]]),
                    ["XQS"], [XQTT_[r]], "c.xq")
                vop((lambda t_, d_: lambda e: e.tensor_tensor(d_[:, 0:1], smalls[:, t_:t_ + 1],
                                                              smalls[:, 34:35], ALU.mult))(tt, dcol),
                    [SM], [DCOL], "c.d1")
                vop((lambda t_, d_: lambda e: e.tensor_tensor(d_[:, 1:2], smalls[:, t_:t_ + 1],
                                                              smalls[:, 37:38], ALU.mult))(tt, dcol),
                    [SM], [DCOL], "c.d2")
                yield
                for hc in range(NC):
                    top((lambda h_, r_: lambda e: e.matmul(
                        psf[4][:], xqtt[r_][:, h_ * 128:(h_ + 1) * 128],
                        wgT[:, h_ * 512:(h_ + 1) * 512],
                        start=(h_ == 0), stop=(h_ == NC - 1)))(hc, r),
                        [XQTT_[r], "WGT"], [PSF[4]], "c.gmm")
                    top((lambda h_, r_: lambda e: e.matmul(
                        psf[5][:], xqtt[r_][:, h_ * 128:(h_ + 1) * 128],
                        wuT[:, h_ * 512:(h_ + 1) * 512],
                        start=(h_ == 0), stop=(h_ == NC - 1)))(hc, r),
                        [XQTT_[r], "WUT"], [PSF[5]], "c.umm")
                sop((lambda s_, d_: lambda e: e.activation(s_[:], psf[4][:], AF.Silu,
                                                           scale=d_[:, 0:1]))(sg, dcol),
                    [PSF[4], DCOL], [SG], "c.si")
                vop((lambda i_, s_, d_: lambda e: e.scalar_tensor_tensor(
                    i_[:], psf[5][:], d_[:, 1:2],
                    s_[:], ALU.mult, ALU.mult))(inter_sb, sg, dcol),
                    [PSF[5], DCOL, SG], [INTS], "c.up")
                dma("pool", AP(s0_d, tt * 128 * I + ob * 1024 + half * 512, [[I, 128], [1, 512]]),
                    inter_sb[:], [INTS], [f"S0_{tt}_{ob}h{half}"], "c.sp")
                yield

    gb, gc = gen_B(), gen_C()
    done_b = done_c = False
    while not (done_b and done_c):
        if not done_b:
            try:
                next(gb)
            except StopIteration:
                done_b = True
        for _ in range(1):
            if not done_c:
                try:
                    next(gc)
                except StopIteration:
                    done_c = True
    w.barrier()
    phBC.close()
    phAB.close()

    phC2 = ExitStack()
    bigA2 = [sb(phC2, f"bigA{i}", [128, I], F32) for i in range(2)]
    bigB = sb(phC2, "bigB", [128, I], F32)
    bigC = sb(phC2, "bigC", [128, I], F32)
    q4b = [sb(phC2, f"q4b{i}", [128, D], F32) for i in range(4)]
    yqbf = sb(phC2, "yqbf", [128, I], BF16)
    yqt_sb = sb(phC2, "yqt_sb", [128, I], BF16)
    st22 = [sb(phC2, f"st2{i}", [128, 4], F32) for i in range(2)]

    for tt in range(NT):
        r = tt % 2
        bigA, st2 = bigA2[r], st22[r]
        BIGA, ST2 = f"BIGA{r}", f"ST2{r}"
        dma("sync", bigA[:], AP(s0_d, tt * 128 * I, [[I, 128], [1, I]]),
            [f"S0_{tt}_{ob}h{hf}" for ob in range(8) for hf in range(2)], [BIGA], "f.in")
        for cg in range(16):
            pt_ = cg % 2
            for j in range(4):
                c = cg * 4 + j
                tp(psf[pt_][:, j * 128:(j + 1) * 128], bigA[:, c * 128:(c + 1) * 128],
                   j == 0, j == 3, [BIGA], [PSF[pt_]], "f.tp")
            vop((lambda g_, p_, ba: lambda e: e.tensor_copy(bigB[:, g_ * 512:(g_ + 1) * 512],
                                                            psf[p_][:]))(cg, pt_, bigA),
                [PSF[pt_]], ["BIGB"], "f.ev1")
        for cg in range(16):
            ps_ = 2 + cg % 2
            for j in range(4):
                c = cg * 4 + j
                top((lambda c_, j_, p_: lambda e: e.matmul(
                    psf[p_][:, j_ * 128:(j_ + 1) * 128], h128_s[:],
                    bigB[:, c_ * 128:(c_ + 1) * 128], start=(j_ == 0), stop=(j_ == 3)))(c, j, ps_),
                    ["BIGB", "H128"], [PSF[ps_]], "f.s1")
            vop((lambda g_, p_: lambda e: e.tensor_copy(bigC[:, g_ * 512:(g_ + 1) * 512],
                                                        psf[p_][:]))(cg, ps_),
                [PSF[ps_]], ["BIGC"], "f.ev2")
        base = tt * 64 * 128 * 128
        dma("pool", AP(s1_d, base, [[128 * 128, 64], [128, 64], [1, 128]]),
            AP(bigC, 0, [[I, 64], [128, 64], [1, 128]]),
            ["BIGC"], ["S1A"], "f.sp1")
        dma("pool", AP(s1_d, base + 64 * 128, [[128 * 128, 64], [128, 64], [1, 128]]),
            AP(bigC, 64 * I, [[I, 64], [128, 64], [1, 128]]),
            ["BIGC"], ["S1B"], "f.sp2")
        dma("sync", AP(bigB, 0, [[I, 128], [128, 64], [1, 128]]),
            AP(s1_d, base, [[128, 128], [128 * 128, 64], [1, 128]]),
            ["S1A", "S1B"], ["BIGB"], "f.g")
        for bg in range(16):
            ps_ = 4 + bg % 2
            for j in range(4):
                b_ = bg * 4 + j
                top((lambda b2, j_, p_: lambda e: e.matmul(
                    psf[p_][:, j_ * 128:(j_ + 1) * 128],
                    bigB[:, b2 * 128:(b2 + 1) * 128], m2_s[:],
                    start=(j_ == 0), stop=(j_ == 3)))(b_, j, ps_),
                    ["BIGB", "M2"], [PSF[ps_]], "f.s2")
            vop((lambda g_, p_, ba: lambda e: e.tensor_copy(ba[:, g_ * 512:(g_ + 1) * 512],
                                                            psf[p_][:]))(bg, ps_, bigA),
                [PSF[ps_]], [BIGA], "f.ev3")
        vop((lambda s_, ba: lambda e: e.tensor_reduce(s_[:, 0:1], ba[:], AX.X, ALU.max,
                                                      apply_absolute_value=True))(st2, bigA),
            [BIGA], [ST2], "f.am")
        vop((lambda s_: lambda e: e.tensor_scalar(s_[:, 0:1], s_[:, 0:1], ISQI, EPS,
                                                  ALU.mult, ALU.max))(st2),
            [ST2], [ST2], "f.cl")
        vop((lambda s_: lambda e: e.reciprocal(s_[:, 1:2], s_[:, 0:1]))(st2),
            [ST2], [ST2], "f.rc")
        vop((lambda s_: lambda e: e.tensor_scalar(s_[:, 1:2], s_[:, 1:2], 128.0, None, ALU.mult))(st2),
            [ST2], [ST2], "f.s2s")
        vop((lambda s_: lambda e: e.tensor_scalar(s_[:, 2:3], s_[:, 1:2], ISQI, None, ALU.mult))(st2),
            [ST2], [ST2], "f.sap")
        vop((lambda c_, s_: lambda e: e.tensor_scalar(smalls[:, 16 + c_:17 + c_],
                                                      s_[:, 0:1], 1.0 / 128.0, None,
                                                      ALU.mult))(tt, st2),
            [ST2], [SM], "f.dqa")
        vop((lambda c_: lambda e: e.tensor_tensor(smalls[:, 16 + c_:17 + c_],
                                                  smalls[:, 16 + c_:17 + c_],
                                                  smalls[:, 40:41], ALU.mult))(tt),
            [SM], [SM], "f.dqb")
        for sl in range(4):
            quant_ladder(bigA[:, sl * D:(sl + 1) * D], [BIGA], st2[:, 2:3], [ST2],
                         yqbf[:, sl * D:(sl + 1) * D], ["YQBF"],
                         [q4b[0][:], q4b[1][:], q4b[2][:], q4b[3][:]], "QSC2",
                         "f.q%d" % sl)
        for bg in range(8):
            for j in range(8):
                b_ = bg * 8 + j
                tp(pbf[0][:, j * 128:(j + 1) * 128], yqbf[:, b_ * 128:(b_ + 1) * 128],
                   j == 0, j == 7, ["YQBF"], [PBF[0]], "f.ytp")
            vop((lambda g_: lambda e: e.tensor_copy(yqt_sb[:, g_ * 1024:(g_ + 1) * 1024],
                                                    pbf[0][:]))(bg),
                [PBF[0]], ["YQT"], "f.yev")
        dma("pool", AP(s2_d, tt * 128, [[128 * T, 64], [T, 64], [1, 128]]),
            AP(yqt_sb, 0, [[I, 64], [128, 64], [1, 128]]),
            ["YQT"], [f"S2_{tt}a"], "f.ysp1")
        dma("pool", AP(s2_d, 64 * T + tt * 128, [[128 * T, 64], [T, 64], [1, 128]]),
            AP(yqt_sb, 64 * I, [[I, 64], [128, 64], [1, 128]]),
            ["YQT"], [f"S2_{tt}b"], "f.ysp2")
    w.barrier()
    phC2.close()

    phD = ExitStack()
    wdT = sb(phD, "wdT", [128, NI * 512], BF16)
    wo_blk = sb(phD, "wo_blk", [128, NC * 512], BF16)
    wdld = [sb(phD, "wdld0", [128, I], F32)]
    WDLD = ["WDLD0"]
    qa2 = sb(phD, "qa2", [128, I // 2], BF16)
    qb2 = sb(phD, "qb2", [128, I // 2], BF16)
    wdq = sb(phD, "wdq", [128, I // 2], BF16)
    yq_tt = [sb(phD, f"yqtt{i}", [128, NI * 128], BF16) for i in range(2)]
    YQTT = ["YQTT0", "YQTT1"]
    pre_tt = [sb(phD, f"prett{i}", [128, H * 128], BF16) for i in range(2)]
    PRETT = ["PRETT0", "PRETT1"]
    xsl = [sb(phD, f"xsl{i}", [128, 512], F32) for i in range(2)]
    XSL = ["XSL0", "XSL1"]
    osb_ = [sb(phD, f"osb{i}", [128, 512], F32) for i in range(2)]
    osb2_ = [sb(phD, f"osb2{i}", [128, 512], F32) for i in range(2)]

    for hb in range(4):
        dma("sync", wo_blk[:],
            AP(wob_d, hb * 512, [[D, 128], [128 * D, NC], [1, 512]]),
            [f"WOB{c}" for c in range(NC)], ["WOBLK"], "d.wo")
        for r in range(4):
            bu = 0
            row0 = hb * 512 + r * 128
            dma("sync", wdld[bu][:], AP(wd_d, row0 * I, [[I, 128], [1, I]]),
                [], [WDLD[bu]], "d.wl")
            for half in range(2):
                hof = half * (I // 2)
                vop((lambda b_, o_: lambda e: e.tensor_scalar(qa2[:], wdld[b_][:, o_:o_ + I // 2],
                                                              smalls[:, 38:39], 0.5,
                                                              ALU.mult, ALU.is_ge))(bu, hof),
                    [WDLD[bu], SM], ["QA2"], "d.ta")
                vop((lambda b_, o_: lambda e: e.tensor_scalar(qb2[:], wdld[b_][:, o_:o_ + I // 2],
                                                              smalls[:, 39:40], 0.5,
                                                              ALU.mult, ALU.is_ge))(bu, hof),
                    [WDLD[bu], SM], ["QB2"], "d.tb")
                vop(lambda e: e.tensor_tensor(wdq[:], qa2[:], qb2[:], ALU.subtract),
                    ["QA2", "QB2"], ["WDQ"], "d.tq")
                for fg in range(4):
                    for j in range(8):
                        fcl = fg * 8 + j
                        tp(pbf[half][:, j * 128:(j + 1) * 128], wdq[:, fcl * 128:(fcl + 1) * 128],
                           j == 0, j == 7, ["WDQ"], [PBF[half]], "d.tp")
                    vop((lambda g_, r_, p_: lambda e: e.tensor_copy(
                        AP(wdT, (g_ * 8) * 512 + r_ * 128, [[NI * 512, 128], [512, 8], [1, 128]]),
                        p_[:]))(fg + half * 4, r, pbf[half]),
                        [PBF[half]], ["WDT"], "d.ev")
        for tt in range(NT):
            bu = tt % 2
            bp = tt % 2
            osb, osb2 = osb_[bu], osb2_[bu]
            OSB, OSB2 = f"OSB{bu}", f"OSB2{bu}"
            pdn, pwo = 2 * bu, 2 * bu + 1
            dma("sync", yq_tt[bu][:],
                AP(s2_d, tt * 128, [[T, 128], [128 * T, NI], [1, 128]]),
                [f"S2_{t2}{sfx}" for t2 in range(NT) for sfx in "ab"] if hb == 0 and tt == 0 else
                [f"S2_{tt}a", f"S2_{tt}b"],
                [YQTT[bu]], "d.yq")
            dma("sync", pre_tt[bp][:],
                AP(pre_d, tt * 128, [[T, 128], [128 * T, H], [1, 128]]),
                [f"PRE{h2}" for h2 in range(H)] if hb == 0 and tt == 0 else [],
                [PRETT[bp]], "d.pre")
            dma("scalar", xsl[bu][:],
                AP(x_d, tt * 128 * D + hb * 512, [[D, 128], [1, 512]]),
                [], [XSL[bu]], "d.x")
            for fc in range(NI):
                top((lambda f_, b_, p_: lambda e: e.matmul(
                    psf[p_][:], yq_tt[b_][:, f_ * 128:(f_ + 1) * 128],
                    wdT[:, f_ * 512:(f_ + 1) * 512],
                    start=(f_ == 0), stop=(f_ == NI - 1)))(fc, bu, pdn),
                    [YQTT[bu], "WDT"], [PSF[pdn]], "d.dmm")
            for hc in range(NC):
                top((lambda h_, b_, p_: lambda e: e.matmul(
                    psf[p_][:], pre_tt[b_][:, h_ * 128:(h_ + 1) * 128],
                    wo_blk[:, h_ * 512:(h_ + 1) * 512],
                    start=(h_ == 0), stop=(h_ == NC - 1)))(hc, bp, pwo),
                    [PRETT[bp], "WOBLK"], [PSF[pwo]], "d.omm")
            vop((lambda t_, b_, o_, p_: lambda e: e.scalar_tensor_tensor(
                o_[:], psf[p_][:], smalls[:, 16 + t_:17 + t_], xsl[b_][:],
                ALU.mult, ALU.add))(tt, bu, osb, pdn),
                [PSF[pdn], SM, XSL[bu]], [OSB], "d.o1")
            vop((lambda o1, o2, p_: lambda e: e.tensor_tensor(o2[:], o1[:], psf[p_][:], ALU.add))(osb, osb2, pwo),
                [OSB, PSF[pwo]], [OSB2], "d.o2")
            dma("pool", AP(out_d, tt * 128 * D + hb * 512, [[D, 128], [1, 512]]),
                osb2[:], [OSB2], ["OUT"], "d.out")
    w.barrier()
    phD.close()

    w.emit(ctx)
    nc._ctx_keepalive = ctx
    return nc


_NC_CACHE = []


def kernel(L0_attn_x, L0_attn_gamma, L0_attn_Wq, L0_attn_Wkv, L0_attn_Wo,
           L0_ffn_w_gate, L0_ffn_w_up, L0_ffn_w_down):
    consts = build_consts()
    if not _NC_CACHE:
        _NC_CACHE.append(build_program())
    nc = _NC_CACHE[0]
    x = np.ascontiguousarray(np.asarray(L0_attn_x, dtype=np.float32))
    common = {
        "gamma": np.ascontiguousarray(np.asarray(L0_attn_gamma, np.float32)),
        "Wq": np.ascontiguousarray(np.asarray(L0_attn_Wq, np.float32)),
        "Wkv": np.ascontiguousarray(np.asarray(L0_attn_Wkv, np.float32)),
        "Wo": np.ascontiguousarray(np.asarray(L0_attn_Wo, np.float32)),
        "wg": np.ascontiguousarray(np.asarray(L0_ffn_w_gate, np.float32)),
        "wu": np.ascontiguousarray(np.asarray(L0_ffn_w_up, np.float32)),
        "wd": np.ascontiguousarray(np.asarray(L0_ffn_w_down, np.float32)),
    }
    common.update(consts)
    in_maps = []
    for b in range(B):
        m = dict(common)
        m["x"] = np.ascontiguousarray(x[b])
        in_maps.append(m)
    res = run_bass_kernel_spmd(nc, in_maps, core_ids=list(range(B)))
    out = np.stack([np.asarray(res.results[b]["out"], np.float32) for b in range(B)],
                   axis=0)
    return out
